# version 06
# baseline (speedup 1.0000x reference)
import contextlib
import math
import numpy as np
import ml_dtypes
import concourse.bass as bass
import concourse.mybir as mybir
from concourse.bass_utils import run_bass_kernel_spmd

F32 = mybir.dt.float32
BF16 = mybir.dt.bfloat16
I32 = mybir.dt.int32
U8 = mybir.dt.uint8
AF = mybir.ActivationFunctionType
ALU = mybir.AluOpType
AX = mybir.AxisListType

S = 4096
DM = 2048
NHEAD = 16
DFF = 8192
DIN = 4096
MH = 64
CH = 256
NCH = S // CH
DPROJ = 8512
SM_SCALE = 1.0 / math.sqrt(128.0)
EPS = 1e-5

DEBUG = False
PHASES = 99


def T(sem, k=1):
    sem.tick(k)
    return (sem, k)


class Sem:
    __slots__ = ("h", "n")

    def __init__(self, h):
        self.h = h
        self.n = 0

    def tick(self, k=1):
        self.n += k
        return self.n


class Prog:
    def __init__(self, nc, stack):
        self.nc = nc
        self.stack = stack
        self.q = {e: [] for e in ("t", "v", "s", "g", "y")}
        self.lastwait = {e: {} for e in ("t", "v", "s", "g", "y")}
        self.serial = {}

    def sem(self, name):
        return Sem(self.stack.enter_context(self.nc.semaphore(name)))

    def emit(self, eng, fn):
        self.q[eng].append(fn)

    def wait(self, eng, sem, val):
        if val <= 0:
            return
        lw = self.lastwait[eng]
        if lw.get(id(sem), -1) >= val:
            return
        lw[id(sem)] = val
        self.emit(eng, lambda e, _h=sem.h, _v=val: e.wait_ge(_h, _v))

    def _op(self, eng, meth, *args, inc=None, **kw):
        if inc is None:
            incs = []
        elif isinstance(inc, list):
            incs = list(inc)
        else:
            incs = [inc]
        ser = self.serial.get(eng)
        if ser is not None and meth != "dma_start":
            if eng != "t":
                has = any(s is ser for s, _ in incs)
                if not has:
                    ser.tick()
                    incs.append((ser, 1))
        def f(e, m=meth, a=args, k=kw, ic=tuple(incs)):
            i = getattr(e, m)(*a, **k)
            for sem, amt in ic:
                i.then_inc(sem.h, amt)
        self.emit(eng, f)

    def mm(self, out, lhsT, rhs, start, stop, inc=None):
        self._op("t", "matmul", out, lhsT, rhs, start=start, stop=stop, inc=inc)

    def tp(self, out, in_, ident, inc=None):
        self._op("t", "transpose", out, in_, ident, inc=inc)

    def nop_t(self, inc):
        self._op("t", "nop", inc=inc)

    def cp(self, out, in_, inc=None, eng="v"):
        self._op(eng, "tensor_copy", out, in_, inc=inc)

    def tt(self, out, in0, in1, op, inc=None, eng="v"):
        self._op(eng, "tensor_tensor", out, in0, in1, op, inc=inc)

    def ts(self, out, in0, s1, s2=None, op0=ALU.mult, op1=None, inc=None):
        if op1 is None:
            self._op("v", "tensor_scalar", out, in0, s1, None, op0, inc=inc)
        else:
            self._op("v", "tensor_scalar", out, in0, s1, s2, op0, op1, inc=inc)

    def red(self, out, in_, op, inc=None):
        self._op("v", "tensor_reduce", out, in_, AX.X, op, inc=inc)

    def recip(self, out, in_, inc=None):
        self._op("v", "reciprocal", out, in_, inc=inc)

    def memset(self, ap, val, inc=None, eng="v"):
        self._op(eng, "memset", ap, val, inc=inc)

    def ata(self, out, in0, in1, scale, bias, inc=None):
        self._op("v", "affine_then_add", out, in0, in1, scale, bias, inc=inc)

    def act(self, out, in_, func, bias=None, scale=None, accum_out=None, inc=None):
        kw = {}
        if bias is not None:
            kw["bias"] = bias
        if scale is not None:
            kw["scale"] = scale
        if accum_out is not None:
            kw["accum_out"] = accum_out
        self._op("s", "activation", out, in_, func, inc=inc, **kw)

    def dma(self, queue, out, in_, sem, slow=False, accum=False):
        kw = {}
        if slow:
            kw["allow_slow_non_contiguous"] = True
        if accum:
            kw["accum_op"] = ALU.add
        self._op(queue, "dma_start", out, in_, inc=(sem, 16), **kw)
        return sem.tick(16)


class Ring:

    def __init__(self, P, name, nslots, queue):
        self.P = P
        self.queue = queue
        self.sems = [P.sem(f"{name}{i}") for i in range(nslots)]
        self.consumed = [None] * nslots
        self.nslots = nslots
        self.nload = 0
        self.mark_i = 0

    def load(self, pairs, slow=False, fill=None):
        s = self.nload % self.nslots
        if self.consumed[s] is not None:
            self.P.wait(self.queue, *self.consumed[s])
        v = 0
        for dst, srcin in pairs:
            if fill is not None:
                self.P._op(self.queue, "memset", dst, fill)
            v = self.P.dma(self.queue, dst, srcin, self.sems[s], slow=slow,
                           accum=(fill is not None))
        self.nload += 1
        return (self.sems[s], v)

    def mark(self, semval):
        self.consumed[self.mark_i % self.nslots] = semval
        self.mark_i += 1


class StoreTrack:

    def __init__(self, P, name, nslots, queue="g"):
        self.P = P
        self.queue = queue
        self.sems = [P.sem(f"{name}{i}") for i in range(nslots)]
        self.vals = [0] * nslots
        self.n = 0
        self.gi = 0

    def guard(self, eng):
        s = self.gi % len(self.sems)
        self.gi += 1
        if self.vals[s]:
            self.P.wait(eng, self.sems[s], self.vals[s])
        return s

    def store(self, dst, srcin):
        s = self.n % len(self.sems)
        self.vals[s] = self.P.dma(self.queue, dst, srcin, self.sems[s])
        self.n += 1


def build_nc():
    nc = bass.Bass("TRN2", target_bir_lowering=False)
    stack = contextlib.ExitStack()
    skind = "ExternalOutput" if DEBUG else "Internal"

    def din(name, shape, dt):
        return nc.dram_tensor(name, shape, dt, kind="ExternalInput").ap()

    x_in = din("x_in", [DM, S], F32)
    wqT = din("wqT", [DM, DM], F32)
    wkT = din("wkT", [DM, DM], F32)
    wvT = din("wvT", [DM, DM], F32)
    woT = din("woT", [DM, DM], F32)
    b_qkvo = din("b_qkvo", [4, 16, 128, 1], F32)
    ropeqc_d = din("ropeqc", [128, S], BF16)
    ropeqs_d = din("ropeqs", [128, S], BF16)
    ropekc_d = din("ropekc", [128, S], BF16)
    ropeks_d = din("ropeks", [128, S], BF16)
    swapm_d = din("swapm", [128, 128], BF16)
    winT = din("winT", [DM, DPROJ], F32)
    convw_d = din("convw", [34, 128, 4], F32)
    convb_d = din("convb", [34, 128, 1], F32)
    dtbias_d = din("dtbias", [64, 1], F32)
    alog_d = din("alog", [64, 1], F32)
    dp_d = din("dp", [32, 128, 1], F32)
    normw_d = din("normw", [32, 128, 1], F32)
    woutT = din("woutT", [DIN, DM], F32)
    g8_d = din("g8", [64, 128, 2048], U8)
    u8_d = din("u8", [64, 128, 2048], U8)
    dn8_d = din("dn8", [16, 128, 8192], U8)
    gsc_d = din("gsc", [64, 128, 1], F32)
    usc_d = din("usc", [64, 128, 1], F32)
    dsc_d = din("dsc", [16, 128, 1], F32)
    identb_d = din("identb", [128, 128], BF16)
    identf_d = din("identf", [128, 128], F32)
    onesf_d = din("onesf", [1, 128], F32)
    fmask_d = din("fmask", [4, 128, 512], BF16)
    cmask_d = din("cmask", [2, 128, 256], BF16)

    out_xT = nc.dram_tensor("out_xT", [DM, S], F32, kind="ExternalOutput").ap()

    def dscr(name, shape, dt):
        return nc.dram_tensor(name, shape, dt, kind=skind).ap()

    xres = dscr("xres", [DM, S], F32)
    qT_d = dscr("qT", [DM, S], BF16)
    kT_d = dscr("kT", [DM, S], BF16)
    vtok_d = dscr("vtok", [NHEAD, 32, 128, 128], BF16)
    ctx_d = dscr("ctx", [DM, S], BF16)
    zT_d = dscr("zT", [DIN, S], BF16)
    xssdT_d = dscr("xssdT", [DIN, S], BF16)
    bT_d = dscr("bT", [128, S], BF16)
    cT_d = dscr("cT", [128, S], BF16)
    yT_d = dscr("yT", [DIN, S], BF16)

    with stack:
        def sbuf(name, shape, dt):
            return stack.enter_context(nc.sbuf_tensor(name, shape, dt))

        A = sbuf("A", [128, 65536], BF16)
        B = sbuf("Bb", [128, 24576], BF16)
        WT = sbuf("WT", [128, 8192], BF16)
        identb = sbuf("identb_s", [128, 128], BF16)
        identf = sbuf("identf_s", [128, 128], F32)
        onesf = sbuf("onesf_s", [1, 128], F32)
        onesb = sbuf("onesb", [128, 128], BF16)
        fmask = sbuf("fmask_s", [128, 4 * 512], BF16)
        cmask = sbuf("cmask_s", [128, 2 * 256], BF16)
        biasq = sbuf("biasq", [128, 4 * 16], F32)
        scales = sbuf("scales", [128, 144], F32)
        convw = sbuf("convw_s", [128, 34 * 4], F32)
        convb = sbuf("convb_s", [128, 34], F32)
        mparam = sbuf("mparam", [64, 8], F32)
        dp_c = sbuf("dp_c", [128, 32], F32)
        normw_c = sbuf("normw_c", [128, 32], F32)
        small = sbuf("small", [128, 512], F32)
        swapm = sbuf("swapm_s", [128, 128], BF16)

        PS = stack.enter_context(nc.psum_tensor("PS", [128, 4096], F32))

        def bank(b, n=512, p=128, off=0):
            return PS[0:p, b * 512 + off: b * 512 + off + n]

        def vA(c0, p, n, dt):
            if dt == BF16:
                return A[0:p, c0:c0 + n]
            return A[0:p, c0:c0 + 2 * n].bitcast(F32)

        def vB(c0, p, n, dt):
            if dt == BF16:
                return B[0:p, c0:c0 + n]
            return B[0:p, c0:c0 + 2 * n].bitcast(F32)

        P = Prog(nc, stack)
        s_ld = P.sem("s_ld")
        s_cst = P.sem("s_cst")
        s_st = P.sem("s_st")
        s_pe = P.sem("s_pe")
        s_ve = P.sem("s_ve")
        s_se = P.sem("s_se")
        s_wfree = P.sem("s_wfree")
        s_ge = P.sem("s_ge")
        P.serial = {"v": s_ve, "s": s_se, "g": s_ge, "t": s_pe}

        def barrier_loads(queue="y"):
            P.wait(queue, s_pe, s_pe.n)
            P.wait(queue, s_ve, s_ve.n)
            P.wait(queue, s_se, s_se.n)

        P.dma("y", identb[:, :], identb_d, s_ld)
        P.dma("y", swapm[:, :], swapm_d, s_ld)
        P.dma("y", identf[:, :], identf_d, s_ld)
        P.dma("y", onesf[:, :], onesf_d, s_ld)
        P.dma("y", fmask[:, :].rearrange("p (r m) -> p r m", r=4),
              fmask_d.rearrange("r p m -> p r m"), s_ld)
        P.dma("y", cmask[:, :].rearrange("p (r m) -> p r m", r=2),
              cmask_d.rearrange("r p m -> p r m"), s_ld)
        P.dma("y", biasq[:, :].unsqueeze(2),
              b_qkvo.rearrange("w t p u -> p (w t) u"), s_ld, slow=True)
        P.dma("y", scales[:, 0:64].unsqueeze(2),
              gsc_d.rearrange("t p u -> p t u"), s_ld, slow=True)
        P.dma("y", scales[:, 64:128].unsqueeze(2),
              usc_d.rearrange("t p u -> p t u"), s_ld, slow=True)
        P.dma("y", scales[:, 128:144].unsqueeze(2),
              dsc_d.rearrange("t p u -> p t u"), s_ld, slow=True)
        P.dma("y", convw[:, :].rearrange("p (t w) -> p t w", t=34),
              convw_d.rearrange("t p w -> p t w"), s_ld)
        P.dma("y", convb[:, :].unsqueeze(2),
              convb_d.rearrange("t p u -> p t u"), s_ld, slow=True)
        P.dma("y", mparam[:, 0:1], dtbias_d, s_ld)
        P.dma("y", mparam[:, 1:2], alog_d, s_ld)
        P.dma("y", dp_c[:, :].unsqueeze(2),
              dp_d.rearrange("t p u -> p t u"), s_ld, slow=True)
        v_consts = P.dma("y", normw_c[:, :].unsqueeze(2),
                         normw_d.rearrange("t p u -> p t u"), s_ld, slow=True)

        P.wait("v", s_ld, v_consts)
        P.memset(onesb[:, :], 1.0, inc=T(s_ve))
        P.wait("s", s_ld, v_consts)
        P.act(mparam[:, 2:3], mparam[:, 1:2], AF.Exp, inc=T(s_se))
        P.wait("v", s_se, s_se.n)
        P.ts(mparam[:, 2:3], mparam[:, 2:3], -1.0, inc=T(s_ve))

        ln_ring = Ring(P, "lnst", 2, "y")

        def ln_phase(src_d, store_ready):
            xn = [A[:, t * S:(t + 1) * S] for t in range(16)]
            stg = [vB(0, 128, 2048, F32), vB(4096, 128, 2048, F32)]
            sq = vB(8192, 128, S, BF16)
            mean_bf = vB(12288, 128, S, BF16)
            var_bf = vB(16384, 128, S, BF16)
            m2 = vB(20480, 128, S, BF16)
            barrier_loads("y")
            if store_ready:
                P.wait("y", s_st, store_ready)
            cp_done = {}
            for t in range(16):
                for h in range(2):
                    sl = ln_ring.nload % 2
                    rdy = ln_ring.load([(stg[sl],
                        src_d[t * 128:(t + 1) * 128, h * 2048:(h + 1) * 2048])])
                    P.wait("v", *rdy)
                    P.cp(xn[t][:, h * 2048:(h + 1) * 2048], stg[sl],
                         inc=T(s_ve))
                    cp_done[2 * t + h] = s_ve.n
                    ln_ring.mark((s_ve, s_ve.n))
            for t in range(16):
                P.wait("t", s_ve, cp_done[2 * t + 1])
                for sc in range(8):
                    P.mm(bank(sc), onesb[:, :], xn[t][:, sc * 512:(sc + 1) * 512],
                         start=(t == 0), stop=(t == 15),
                         inc=T(s_pe) if (t == 15 and sc == 7) else None)
            v_sums = s_pe.n
            P.wait("v", s_pe, v_sums)
            for sc in range(8):
                P.ts(mean_bf[:, sc * 512:(sc + 1) * 512], bank(sc), 1.0 / DM,
                     inc=T(s_ve) if sc == 7 else None)
            v_mean = s_ve.n
            sq_mm = {}
            for t in range(16):
                if t >= 1:
                    P.wait("v", s_pe, sq_mm[t - 1])
                P.tt(sq[:, :], xn[t], xn[t], ALU.mult, inc=T(s_ve))
                P.wait("t", s_ve, s_ve.n)
                if t == 0:
                    P.wait("t", s_ve, v_mean)
                for sc in range(8):
                    P.mm(bank(sc), onesb[:, :], sq[:, sc * 512:(sc + 1) * 512],
                         start=(t == 0), stop=(t == 15),
                         inc=T(s_pe) if sc == 7 else None)
                sq_mm[t] = s_pe.n
            v_sq = s_pe.n
            P.wait("v", s_pe, v_sq)
            P.tt(m2[:, :], mean_bf[:, :], mean_bf[:, :], ALU.mult)
            for sc in range(8):
                P.ts(var_bf[:, sc * 512:(sc + 1) * 512], bank(sc), 1.0 / DM)
            P.tt(var_bf[:, :], var_bf[:, :], m2[:, :], ALU.subtract)
            P.ts(var_bf[:, :], var_bf[:, :], float(EPS), None, ALU.add,
                 inc=T(s_ve))
            v_var = s_ve.n
            P.wait("s", s_ve, v_var)
            varf = vB(0, 128, S, F32)
            tnewt = vB(8192, 128, 2048, F32)
            P.act(var_bf[:, :], var_bf[:, :], AF.Ln)
            P.act(var_bf[:, :], var_bf[:, :], AF.Exp, scale=-0.5,
                  inc=T(s_se))
            v_rstd0 = s_se.n
            P.wait("v", s_se, v_rstd0)
            for sc in range(8):
                P.ts(varf[:, sc * 512:(sc + 1) * 512], bank(sc), 1.0 / DM)
            P.tt(varf[:, :], varf[:, :], m2[:, :], ALU.subtract)
            P.ts(varf[:, :], varf[:, :], float(EPS), None, ALU.add)
            for sh2 in range(2):
                h0, h1 = sh2 * 2048, (sh2 + 1) * 2048
                P.tt(tnewt[:, :], varf[:, h0:h1], var_bf[:, h0:h1], ALU.mult)
                P.tt(tnewt[:, :], tnewt[:, :], var_bf[:, h0:h1], ALU.mult)
                P.ts(tnewt[:, :], tnewt[:, :], -0.5, 1.5, ALU.mult, ALU.add)
                P.tt(var_bf[:, h0:h1], var_bf[:, h0:h1], tnewt[:, :],
                     ALU.mult, inc=T(s_ve) if sh2 == 1 else None)
            v_rstd = s_ve.n
            P.wait("v", s_ve, v_rstd)
            for t in range(16):
                P.tt(xn[t], xn[t], mean_bf[:, :], ALU.subtract)
                P.tt(xn[t], xn[t], var_bf[:, :], ALU.mult,
                     inc=T(s_ve))
            return s_ve.n

        wt_ring = Ring(P, "wtw", 2, "g")
        qk_store = StoreTrack(P, "qks", 2, "g")

        def qkv_phase(v_xn):
            ropeB = vB(6144, 128, 4 * S, BF16)
            P.wait("y", s_ve, v_xn)
            P.dma("y", ropeB[:, 0:S], ropeqc_d, s_ld)
            P.dma("y", ropeB[:, S:2 * S], ropeqs_d, s_ld)
            P.dma("y", ropeB[:, 2 * S:3 * S], ropekc_d, s_ld)
            P.dma("y", ropeB[:, 3 * S:4 * S], ropeks_d, s_ld)
            v_rope_ld = s_ld.n
            xn = [A[:, t * S:(t + 1) * S] for t in range(16)]
            tmp = vB(0, 128, 2048, BF16)
            stage = [vB(2048, 128, 2048, BF16), vB(4096, 128, 2048, BF16)]
            sw_sb = vB(22528, 128, 2048, BF16)
            wslot = [WT[:, 0:2048], WT[:, 2048:4096]]
            evac_done = {}
            gidx = 0
            for wdram, odram, bi, isqk in ((wqT, qT_d, 0, True),
                                           (wkT, kT_d, 1, True),
                                           (wvT, vtok_d, 2, False)):
                for it in range(16):
                    sl = wt_ring.nload % 2
                    w_rdy = wt_ring.load([(wslot[sl]
                        .rearrange("p (c i) -> p c i", c=16),
                        wdram.rearrange("(c p) i -> p c i", p=128)
                        [:, :, it * 128:(it + 1) * 128])])
                    P.wait("t", *w_rdy)
                    P.wait("t", s_ve, v_xn)
                    for sh in range(2):
                        g = gidx
                        gidx += 1
                        bk = (g % 2) * 4
                        if g >= 2:
                            P.wait("t", s_ve, evac_done[g - 2])
                        for d in range(16):
                            lw = wslot[sl][:, d * 128:(d + 1) * 128]
                            for sc in range(4):
                                P.mm(bank(bk + sc), lw,
                                     xn[d][:, sh * 2048 + sc * 512:
                                           sh * 2048 + (sc + 1) * 512],
                                     start=(d == 0), stop=(d == 15),
                                     inc=T(s_pe)
                                     if (d == 15 and sc == 3) else None)
                        v_mm = s_pe.n
                        if sh == 1:
                            wt_ring.mark((s_pe, v_mm))
                        P.wait("v", s_pe, v_mm)
                        psg = PS[:, bk * 512:(bk + 4) * 512]
                        ssl = qk_store.guard("v")
                        qs = stage[ssl]
                        if isqk:
                            P.wait("v", s_ld, v_rope_ld)
                            P.ts(tmp[:, :], psg, biasq[:, bi * 16 + it:
                                                       bi * 16 + it + 1],
                                 None, ALU.add, inc=T(s_ve))
                            v_tmp = s_ve.n
                            P.wait("t", s_ve, v_tmp)
                            for sc in range(4):
                                P.mm(bank(bk + sc), swapm[:, :],
                                     tmp[:, sc * 512:(sc + 1) * 512],
                                     start=True, stop=True,
                                     inc=T(s_pe) if sc == 3 else None)
                            v_sw = s_pe.n
                            co = bi * 2 * S + sh * 2048
                            cosA = ropeB[:, co:co + 2048]
                            sinA = ropeB[:, co + S:co + S + 2048]
                            P.tt(qs[:, :], tmp[:, :], cosA, ALU.mult)
                            P.wait("v", s_pe, v_sw)
                            P.tt(sw_sb[:, :], psg, sinA, ALU.mult)
                            P.tt(qs[:, :], qs[:, :], sw_sb[:, :], ALU.add,
                                 inc=T(s_ve))
                            evac_done[g] = s_ve.n
                            P.wait("g", s_ve, s_ve.n)
                            qk_store.store(odram[it * 128:(it + 1) * 128,
                                                 sh * 2048:(sh + 1) * 2048], qs)
                        else:
                            P.ts(tmp[:, :], psg, biasq[:, 32 + it:32 + it + 1],
                                 None, ALU.add, inc=T(s_ve))
                            v_bias = s_ve.n
                            P.wait("t", s_ve, v_bias)
                            pbt = PS[:, bk * 512:(bk + 2) * 512].bitcast(BF16)
                            for k in range(16):
                                P.tp(pbt[:, k * 128:(k + 1) * 128],
                                     tmp[:, k * 128:(k + 1) * 128], identb[:, :],
                                     inc=T(s_pe) if k == 15 else None)
                            v_tp = s_pe.n
                            P.wait("v", s_pe, v_tp)
                            P.cp(qs[:, :], pbt[:, 0:2048], inc=T(s_ve))
                            evac_done[g] = s_ve.n
                            P.wait("g", s_ve, s_ve.n)
                            qk_store.store(vtok_d[it, sh * 16:(sh + 1) * 16]
                                           .rearrange("c p m -> p c m"),
                                           qs[:, :].rearrange("p (c m) -> p c m",
                                                              c=16))
            v_all = s_st.n
            for s2 in range(2):
                if qk_store.vals[s2]:
                    P.wait("y", qk_store.sems[s2], qk_store.vals[s2])
            return 0

        fk_ring = Ring(P, "fkqv", 2, "y")
        fc_store = StoreTrack(P, "fctx", 4, "g")

        def flash_phase():
            scores = vA(0, 128, 16384, F32)
            expS = vA(32768, 128, 16384, BF16)
            maxacc = vA(49152, 128, 512, F32)
            perq = vA(50176, 128, 1, F32)
            negm = vA(50180, 128, 1, F32)
            m11 = vA(50184, 1, 8, F32)
            rrow = vA(50208, 1, 512, F32)
            rb_sb = vA(51232, 128, 512, F32)
            cstage = [vA(52256 + i * 512, 128, 512, BF16) for i in range(4)]
            kqv = [dict(K=vB(sl * 12288, 128, S, BF16),
                        Q=vB(sl * 12288 + 4096, 128, S, BF16),
                        V=vB(sl * 12288 + 8192, 128, S, BF16)) for sl in range(2)]
            barrier_loads("y")
            hload = {}

            def load_head(hh):
                sl2 = fk_ring.nload % 2
                hload[hh] = fk_ring.load([
                    (kqv[sl2]["K"], kT_d[hh * 128:(hh + 1) * 128, :]),
                    (kqv[sl2]["Q"], qT_d[hh * 128:(hh + 1) * 128, :]),
                    (kqv[sl2]["V"], vtok_d[hh].rearrange("c p m -> p c m"))])

            load_head(0)
            load_head(1)
            qk_copy = {}
            v_ctx_read = 0
            prev_maxdone = 0
            prev_exp_reads = 0
            prev_pv = 0
            it_idx = 0
            for h in range(NHEAD):
                sl = h % 2
                K_, Q_, V_ = kqv[sl]["K"], kqv[sl]["Q"], kqv[sl]["V"]
                P.wait("t", *hload[h])
                for qb in range(8):
                    nck = 4 * (qb + 1)
                    P.wait("s", s_ve, prev_maxdone)
                    for kc in range(nck):
                        bk = kc % 4
                        if bk in qk_copy:
                            P.wait("t", s_se, qk_copy[bk])
                        P.mm(bank(bk), K_[:, kc * 128:(kc + 1) * 128],
                             Q_[:, qb * 512:(qb + 1) * 512], start=True, stop=True,
                             inc=T(s_pe))
                        P.wait("s", s_pe, s_pe.n)
                        P.act(scores[:, kc * 512:(kc + 1) * 512], bank(bk), AF.Copy,
                              inc=T(s_se))
                        qk_copy[bk] = s_se.n
                        P.wait("v", s_se, s_se.n)
                        if kc == 0:
                            P.ts(maxacc[:, :], scores[:, 0:512], 1.0)
                        else:
                            P.tt(maxacc[:, :], maxacc[:, :],
                                 scores[:, kc * 512:(kc + 1) * 512], ALU.max,
                                 inc=T(s_ve) if kc == nck - 1 else None)
                    prev_maxdone = s_ve.n
                    P.wait("t", s_ve, prev_maxdone)
                    for slc in range(4):
                        P.mm(bank(6, n=128, off=slc * 128),
                             maxacc[:, slc * 128:(slc + 1) * 128], identf[:, :],
                             start=(slc == 0), stop=(slc == 3),
                             inc=T(s_pe) if slc == 3 else None)
                    P.wait("v", s_pe, s_pe.n)
                    P.red(perq[:, :], bank(6), ALU.max, inc=T(s_ve))
                    P.wait("t", s_ve, s_ve.n)
                    P.mm(bank(4, n=128, p=1), perq[:, :], identf[:, :],
                         start=True, stop=True, inc=T(s_pe))
                    P.wait("v", s_pe, s_pe.n)
                    P.red(m11[0:1, 0:1], bank(4, n=128, p=1), ALU.max)
                    P.ts(m11[0:1, 1:2], m11[0:1, 0:1], -1.0, inc=T(s_ve))
                    v_m11 = s_ve.n
                    P.wait("t", s_ve, v_m11)
                    P.mm(bank(4, n=1, off=256), onesf[:, :], m11[0:1, 1:2],
                         start=True, stop=True, inc=T(s_pe))
                    P.wait("v", s_pe, s_pe.n)
                    P.cp(negm[:, :], bank(4, n=1, off=256), inc=T(s_ve))
                    v_negm = s_ve.n
                    P.wait("s", s_ve, v_negm)
                    P.wait("s", s_ve, prev_exp_reads)
                    P.wait("s", s_pe, prev_pv)
                    for gch in range(qb + 1):
                        P.act(expS[:, gch * 2048:(gch + 1) * 2048],
                              scores[:, gch * 2048:(gch + 1) * 2048], AF.Exp,
                              bias=negm[:, :],
                              inc=T(s_se) if gch == qb else None)
                    v_exp = s_se.n
                    P.wait("v", s_se, v_exp)
                    for dk in range(4):
                        kc = qb * 4 + dk
                        P.tt(expS[:, kc * 512:(kc + 1) * 512],
                             expS[:, kc * 512:(kc + 1) * 512],
                             fmask[:, dk * 512:(dk + 1) * 512], ALU.mult,
                             inc=T(s_ve) if dk == 3 else None)
                    v_mask = s_ve.n
                    P.wait("t", s_ve, v_mask)
                    P.wait("t", s_ve, v_ctx_read)
                    last_of_head = (qb == 7)
                    for kc in range(nck):
                        P.mm(bank(5), V_[:, kc * 128:(kc + 1) * 128],
                             expS[:, kc * 512:(kc + 1) * 512],
                             start=(kc == 0), stop=(kc == nck - 1))
                        P.mm(bank(7, n=512, p=1), onesb[:, 0:1],
                             expS[:, kc * 512:(kc + 1) * 512],
                             start=(kc == 0), stop=(kc == nck - 1),
                             inc=T(s_pe) if kc == nck - 1 else None)
                    prev_pv = s_pe.n
                    if last_of_head:
                        fk_ring.mark((s_pe, prev_pv))
                    prev_exp_reads = v_mask
                    P.wait("v", s_pe, prev_pv)
                    P.recip(rrow[0:1, :], bank(7, n=512, p=1), inc=T(s_ve))
                    v_rr = s_ve.n
                    P.wait("t", s_ve, v_rr)
                    P.mm(bank(4), onesf[:, :], rrow[0:1, :], start=True, stop=True,
                         inc=T(s_pe))
                    v_rb = s_pe.n
                    P.wait("s", s_pe, v_rb)
                    P.act(rb_sb[:, :], bank(4), AF.Copy, inc=T(s_se))
                    v_rbs = s_se.n
                    ssl = fc_store.guard("v")
                    cs = cstage[ssl]
                    P.wait("v", s_se, v_rbs)
                    P.tt(cs[:, :], bank(5), rb_sb[:, :], ALU.mult, inc=T(s_ve))
                    v_ctx_read = s_ve.n
                    P.wait("g", s_ve, v_ctx_read)
                    fc_store.store(ctx_d[h * 128:(h + 1) * 128,
                                         qb * 512:(qb + 1) * 512], cs)
                    it_idx += 1
                nh = h + 2
                if nh < NHEAD:
                    load_head(nh)
            for s2 in range(4):
                if fc_store.vals[s2]:
                    P.wait("y", fc_store.sems[s2], fc_store.vals[s2])
            return 0

        wo_rring = Ring(P, "wor", 2, "y")
        wo_store = StoreTrack(P, "wos", 2, "g")

        def wo_phase(src_resid, dst_resid):
            barrier_loads("y")
            ctx = [A[:, t * S:(t + 1) * S] for t in range(16)]
            for t in range(16):
                P.dma("y", ctx[t], ctx_d[t * 128:(t + 1) * 128, :], s_ld)
            v_ctx = s_ld.n
            rstage = [vB(0, 128, S, F32), vB(8192, 128, S, F32)]
            tmpf = vB(16384, 128, S, F32)
            wslot = [WT[:, 0:2048], WT[:, 2048:4096]]
            evac = {}
            for it in range(16):
                sl = wt_ring.nload % 2
                w_rdy = wt_ring.load([(wslot[sl]
                    .rearrange("p (c i) -> p c i", c=16),
                    woT.rearrange("(c p) i -> p c i", p=128)
                    [:, :, it * 128:(it + 1) * 128])])
                rsl = wo_store.guard("y")
                rs = rstage[rsl]
                if it >= 2:
                    P.wait("y", s_ve, evac[it - 2])
                r_rdy = wo_rring.load([(rs,
                    src_resid[it * 128:(it + 1) * 128, :])])
                P.wait("t", *w_rdy)
                P.wait("t", s_ld, v_ctx)
                if it >= 1:
                    P.wait("t", s_ve, evac[it - 1])
                for sh in range(2):
                    bk = sh * 4
                    for d in range(16):
                        lw = wslot[sl][:, d * 128:(d + 1) * 128]
                        for sc in range(4):
                            P.mm(bank(bk + sc), lw,
                                 ctx[d][:, sh * 2048 + sc * 512:
                                        sh * 2048 + (sc + 1) * 512],
                                 start=(d == 0), stop=(d == 15),
                                 inc=T(s_pe)
                                 if (d == 15 and sc == 3) else None)
                    v_mm = s_pe.n
                    if sh == 1:
                        wt_ring.mark((s_pe, v_mm))
                    P.wait("v", s_pe, v_mm)
                    P.wait("v", *r_rdy)
                    psg = PS[:, bk * 512:(bk + 4) * 512]
                    P.ts(tmpf[:, sh * 2048:(sh + 1) * 2048], psg,
                         biasq[:, 48 + it:48 + it + 1], None, ALU.add)
                    P.tt(rs[:, sh * 2048:(sh + 1) * 2048],
                         rs[:, sh * 2048:(sh + 1) * 2048],
                         tmpf[:, sh * 2048:(sh + 1) * 2048], ALU.add,
                         inc=T(s_ve))
                evac[it] = s_ve.n
                wo_rring.mark((s_ve, evac[it]))
                P.wait("g", s_ve, evac[it])
                wo_store.store(dst_resid[it * 128:(it + 1) * 128, :], rs)
            for s2 in range(2):
                if wo_store.vals[s2]:
                    P.wait("y", wo_store.sems[s2], wo_store.vals[s2])
            return [(wo_store.sems[s2], wo_store.vals[s2]) for s2 in range(2)
                    if wo_store.vals[s2]]

        gus_ring = Ring(P, "gus", 4, "y")
        dns_ring = Ring(P, "dns", 8, "y")
        frs_ring = Ring(P, "frs", 2, "y")
        ffn_store = StoreTrack(P, "ffs", 2, "g")

        def ffn_phase(src_resid, dst_resid, prev_stores):
            xnb = [B[:, t * 1024:(t + 1) * 1024] for t in range(16)]
            hm = [A[:, it * 1024:(it + 1) * 1024] for it in range(64)]
            mean_bf = vB(16384, 128, 1024, BF16)
            var_bf = vB(17408, 128, 1024, BF16)
            m2 = vB(18432, 128, 1024, BF16)
            sqb = vB(19456, 128, 1024, BF16)
            varf = vB(20480, 128, 1024, F32)
            tnewt = vB(22528, 128, 1024, F32)
            rs_ = [vB(20480, 128, 1024, F32), vB(22528, 128, 1024, F32)]
            t1 = vB(18432, 128, 1024, F32)
            sg = [vB(19456, 128, 1024, BF16), vB(18432, 128, 1024, BF16)]
            wslot4 = [WT[:, i * 2048:(i + 1) * 2048] for i in range(4)]
            dnslot = [WT[:, 0:4096], WT[:, 4096:8192]]

            def vBu8(c0, n):
                return B[0:128, c0:c0 + n // 2].bitcast(U8)
            gu_stage = [vBu8(20480 + i * 1024, 2048) for i in range(4)]
            dn_stage = [vBu8(i * 2048, 4096) for i in range(8)]
            for sem, val in prev_stores:
                P.wait("g", sem, val)
                P.wait("y", sem, val)
            barrier_loads("g")
            barrier_loads("y")
            for sb in range(4):
                s0 = sb * 1024
                P.wait("g", s_pe, s_pe.n)
                P.wait("g", s_ve, s_ve.n)
                for t in range(16):
                    P.dma("g", xnb[t],
                          src_resid[t * 128:(t + 1) * 128, s0:s0 + 1024],
                          s_cst)
                v_x = s_cst.n
                P.wait("t", s_cst, v_x)
                P.wait("t", s_ve, s_ve.n)
                for t in range(16):
                    for sc in range(2):
                        P.mm(bank(4 + sc), onesb[:, :],
                             xnb[t][:, sc * 512:(sc + 1) * 512],
                             start=(t == 0), stop=(t == 15),
                             inc=T(s_pe) if (t == 15 and sc == 1) else None)
                v_meanmm = s_pe.n
                P.wait("v", s_pe, v_meanmm)
                for sc in range(2):
                    P.ts(mean_bf[:, sc * 512:(sc + 1) * 512], bank(4 + sc),
                         1.0 / DM, inc=T(s_ve) if sc == 1 else None)
                v_mean = s_ve.n
                sq_mm = {}
                for t in range(16):
                    if t >= 1:
                        P.wait("v", s_pe, sq_mm[t - 1])
                    P.tt(sqb[:, :], xnb[t], xnb[t], ALU.mult, inc=T(s_ve))
                    P.wait("t", s_ve, s_ve.n)
                    if t == 0:
                        P.wait("t", s_ve, v_mean)
                    for sc in range(2):
                        P.mm(bank(4 + sc), onesb[:, :],
                             sqb[:, sc * 512:(sc + 1) * 512],
                             start=(t == 0), stop=(t == 15),
                             inc=T(s_pe) if sc == 1 else None)
                    sq_mm[t] = s_pe.n
                v_sqmm = s_pe.n
                P.wait("v", s_pe, v_sqmm)
                P.tt(m2[:, :], mean_bf[:, :], mean_bf[:, :], ALU.mult)
                for sc in range(2):
                    P.ts(var_bf[:, sc * 512:(sc + 1) * 512], bank(4 + sc),
                         1.0 / DM)
                P.tt(var_bf[:, :], var_bf[:, :], m2[:, :], ALU.subtract)
                P.ts(var_bf[:, :], var_bf[:, :], float(EPS), None, ALU.add,
                     inc=T(s_ve))
                v_var = s_ve.n
                P.wait("s", s_ve, v_var)
                P.act(var_bf[:, :], var_bf[:, :], AF.Ln)
                P.act(var_bf[:, :], var_bf[:, :], AF.Exp, scale=-0.5,
                      inc=T(s_se))
                v_r0 = s_se.n
                P.wait("v", s_se, v_r0)
                for sc in range(2):
                    P.ts(varf[:, sc * 512:(sc + 1) * 512], bank(4 + sc),
                         1.0 / DM)
                P.tt(varf[:, :], varf[:, :], m2[:, :], ALU.subtract)
                P.ts(varf[:, :], varf[:, :], float(EPS), None, ALU.add)
                P.tt(tnewt[:, :], varf[:, :], var_bf[:, :], ALU.mult)
                P.tt(tnewt[:, :], tnewt[:, :], var_bf[:, :], ALU.mult)
                P.ts(tnewt[:, :], tnewt[:, :], -0.5, 1.5, ALU.mult, ALU.add)
                P.tt(var_bf[:, :], var_bf[:, :], tnewt[:, :], ALU.mult,
                     inc=T(s_ve))
                for t in range(16):
                    P.tt(xnb[t], xnb[t], mean_bf[:, :], ALU.subtract)
                    P.tt(xnb[t], xnb[t], var_bf[:, :], ALU.mult,
                         inc=T(s_ve))
                v_xn = s_ve.n
                for s2 in range(2):
                    if ffn_store.vals[s2]:
                        P.wait("y", ffn_store.sems[s2], ffn_store.vals[s2])
                P.wait("y", s_ve, v_xn)
                gu_evac = {}
                wt_mark = [0, 0, 0, 0]
                for it in range(64):
                    srdy = []
                    for w8d in (g8_d, u8_d):
                        ssl = gus_ring.nload % 4
                        rdy = gus_ring.load([(gu_stage[ssl], w8d[it])])
                        srdy.append((ssl, rdy))
                    pp = it % 2
                    bk0 = pp * 4
                    if it >= 2:
                        P.wait("t", s_ve, gu_evac[it - 2])
                    P.wait("t", s_ve, v_xn)
                    for mi in range(2):
                        ssl, rdy = srdy[mi]
                        sl = (2 * it + mi) % 4
                        P.wait("v", *rdy)
                        P.wait("v", s_pe, wt_mark[sl])
                        P.ts(wslot4[sl], gu_stage[ssl], -128.0, None, ALU.add,
                             inc=T(s_ve))
                        v_w = s_ve.n
                        gus_ring.mark((s_ve, v_w))
                        P.wait("t", s_ve, v_w)
                        for d in range(16):
                            lw = wslot4[sl][:, d * 128:(d + 1) * 128]
                            for sc in range(2):
                                P.mm(bank(bk0 + 2 * mi + sc), lw,
                                     xnb[d][:, sc * 512:(sc + 1) * 512],
                                     start=(d == 0), stop=(d == 15),
                                     inc=T(s_pe)
                                     if (d == 15 and sc == 1) else None)
                        wt_mark[sl] = s_pe.n
                    v_mm = s_pe.n
                    P.wait("s", s_pe, v_mm)
                    P.act(sg[pp][:, :], PS[:, bk0 * 512:(bk0 + 2) * 512],
                          AF.Silu, scale=scales[:, it:it + 1], inc=T(s_se))
                    v_sg = s_se.n
                    P.wait("v", s_pe, v_mm)
                    P.ts(hm[it][:, :], PS[:, (bk0 + 2) * 512:(bk0 + 4) * 512],
                         scales[:, 64 + it:65 + it], None, ALU.mult)
                    P.wait("v", s_se, v_sg)
                    P.tt(hm[it][:, :], hm[it][:, :], sg[pp][:, :], ALU.mult,
                         inc=T(s_ve))
                    gu_evac[it] = s_ve.n
                v_gu = s_ve.n
                P.wait("t", s_ve, v_gu)
                P.wait("y", s_pe, s_pe.n)
                dn_evac = {}
                dwt_mark = [0, 0]
                for dmt in range(16):
                    dn_rdy = []
                    for h in range(2):
                        ssl = dns_ring.nload % 8
                        rdy = dns_ring.load([(dn_stage[ssl],
                            dn8_d[dmt][:, h * 4096:(h + 1) * 4096])])
                        dn_rdy.append((ssl, rdy))
                    bko = (dmt % 2) * 2
                    if dmt >= 2:
                        P.wait("t", s_ve, dn_evac[dmt - 2])
                    ffn_store.guard("y")
                    rsl = frs_ring.nload % 2
                    if dmt < 2:
                        P.wait("y", s_ve, v_xn)
                    r_rdy = frs_ring.load([(rs_[rsl],
                        src_resid[dmt * 128:(dmt + 1) * 128, s0:s0 + 1024])])
                    for h in range(2):
                        ssl, rdy = dn_rdy[h]
                        P.wait("v", *rdy)
                        P.wait("v", s_pe, dwt_mark[h])
                        P.ts(dnslot[h], dn_stage[ssl], -128.0, None, ALU.add,
                             inc=T(s_ve))
                        v_w = s_ve.n
                        dns_ring.mark((s_ve, v_w))
                        P.wait("t", s_ve, v_w)
                        for blk in range(32):
                            lw = dnslot[h][:, blk * 128:(blk + 1) * 128]
                            for sc in range(2):
                                P.mm(bank(bko + sc), lw,
                                     hm[h * 32 + blk][:, sc * 512:
                                                      (sc + 1) * 512],
                                     start=(h == 0 and blk == 0),
                                     stop=(h == 1 and blk == 31),
                                     inc=T(s_pe)
                                     if (blk == 31 and sc == 1) else None)
                        dwt_mark[h] = s_pe.n
                    v_mm = s_pe.n
                    P.wait("v", s_pe, v_mm)
                    P.ts(t1[:, :], PS[:, bko * 512:(bko + 2) * 512],
                         scales[:, 128 + dmt:129 + dmt], None, ALU.mult)
                    P.wait("v", *r_rdy)
                    P.tt(rs_[rsl], rs_[rsl], t1[:, :], ALU.add, inc=T(s_ve))
                    dn_evac[dmt] = s_ve.n
                    frs_ring.mark((s_ve, s_ve.n))
                    P.wait("g", s_ve, s_ve.n)
                    ffn_store.store(dst_resid[dmt * 128:(dmt + 1) * 128,
                                              s0:s0 + 1024], rs_[rsl])
            for s2 in range(2):
                if ffn_store.vals[s2]:
                    P.wait("y", ffn_store.sems[s2], ffn_store.vals[s2])
            return [(ffn_store.sems[s2], ffn_store.vals[s2])
                    for s2 in range(2) if ffn_store.vals[s2]]

        win_stage = StoreTrack(P, "wis", 2, "g")

        def win_phase(v_xn):
            xn = [A[:, t * S:(t + 1) * S] for t in range(16)]
            convpad = vB(0, 128, 4104, BF16)
            acca = vB(4160, 128, S, BF16)
            accb = vB(8256, 128, S, BF16)
            stage = [vB(12352, 128, S, BF16), vB(16448, 128, S, BF16)]
            P.wait("v", s_ve, v_xn)
            P.memset(convpad[:, 0:3], 0.0, inc=T(s_ve))
            wslot = [WT[:, 0:2048], WT[:, 2048:4096]]
            evac_done = {}
            g = 0
            v_lastmm = 0
            for it in range(67):
                iw = 128 if it < 66 else 64
                sl = wt_ring.nload % 2
                wsrc = winT.rearrange("(c p) i -> p c i", p=128)
                w_rdy = wt_ring.load([(wslot[sl]
                    .rearrange("p (c i) -> p c i", c=16)[:, :, 0:iw],
                    wsrc[:, :, it * 128:it * 128 + iw])])
                P.wait("t", *w_rdy)
                P.wait("t", s_ve, v_xn)
                for sh in range(2):
                    bk = (g % 2) * 4
                    if g >= 2:
                        P.wait("t", s_ve, evac_done[g - 2])
                    for d in range(16):
                        lw = wslot[sl][:, d * 128:d * 128 + iw]
                        for sc in range(4):
                            P.mm(bank(bk + sc, p=iw), lw,
                                 xn[d][:, sh * 2048 + sc * 512:
                                       sh * 2048 + (sc + 1) * 512],
                                 start=(d == 0), stop=(d == 15),
                                 inc=T(s_pe) if (d == 15 and sc == 3) else None)
                    v_mm = s_pe.n
                    if sh == 1:
                        wt_ring.mark((s_pe, v_mm))
                    v_lastmm = v_mm
                    P.wait("v", s_pe, v_mm)
                    psg = PS[0:iw, bk * 512:(bk + 4) * 512]
                    if it < 32:
                        ssl = win_stage.guard("v")
                        P.cp(stage[ssl][:, 0:2048], psg, inc=T(s_ve))
                        P.wait("g", s_ve, s_ve.n)
                        win_stage.store(zT_d[it * 128:(it + 1) * 128,
                                             sh * 2048:(sh + 1) * 2048],
                                        stage[ssl][:, 0:2048])
                        evac_done[g] = s_ve.n
                    elif it < 66:
                        P.cp(convpad[:, 3 + sh * 2048:3 + (sh + 1) * 2048], psg,
                             inc=T(s_ve))
                        evac_done[g] = s_ve.n
                        if sh == 1:
                            ct = it - 32
                            cwc = lambda w: convw[:, ct * 4 + w:ct * 4 + w + 1]
                            P.ts(acca[:, :], convpad[:, 0:S], cwc(0), None,
                                 ALU.mult)
                            for w in range(1, 4):
                                P.ts(accb[:, :], convpad[:, w:w + S], cwc(w),
                                     None, ALU.mult)
                                P.tt(acca[:, :], acca[:, :], accb[:, :],
                                     ALU.add)
                            P.ts(acca[:, :], acca[:, :], convb[:, ct:ct + 1],
                                 None, ALU.add, inc=T(s_ve))
                            v_conv = s_ve.n
                            P.wait("s", s_ve, v_conv)
                            ssl = win_stage.guard("s")
                            P.act(stage[ssl][:, 0:S], acca[:, :], AF.Silu,
                                  inc=T(s_se))
                            P.wait("g", s_se, s_se.n)
                            if ct < 32:
                                win_stage.store(
                                    xssdT_d[ct * 128:(ct + 1) * 128, :],
                                    stage[ssl][:, 0:S])
                            elif ct == 32:
                                win_stage.store(bT_d, stage[ssl][:, 0:S])
                            else:
                                win_stage.store(cT_d, stage[ssl][:, 0:S])
                            evac_done[g] = s_ve.n
                    else:
                        if sh == 1:
                            dtt = vA(8192, 64, S, F32)
                            bk0 = ((g - 1) % 2) * 4
                            psg0 = PS[0:64, bk0 * 512:(bk0 + 4) * 512]
                            P.ts(dtt[:, 0:2048], psg0,
                                 mparam[:, 0:1], None, ALU.add)
                            P.ts(dtt[:, 2048:4096], psg,
                                 mparam[:, 0:1], None, ALU.add, inc=T(s_ve))
                            evac_done[g] = s_ve.n
                            evac_done[g - 1] = s_ve.n
                    g += 1
            dtt = vA(8192, 64, S, F32)
            daa = vA(0, 64, S, F32)
            dab = vA(16384, 64, S, F32)
            scr = vA(16384, 64, S, F32)
            P.wait("s", s_pe, v_lastmm)
            P.wait("v", s_pe, v_lastmm)
            P.wait("s", s_ve, s_ve.n)
            P.act(daa[:, :], dtt[:, :], AF.Abs, inc=T(s_se))
            P.act(daa[:, :], daa[:, :], AF.Exp, scale=-1.0, inc=T(s_se))
            P.wait("v", s_se, s_se.n)
            P.ts(daa[:, :], daa[:, :], 1.0, None, ALU.add, inc=T(s_ve))
            P.wait("s", s_ve, s_ve.n)
            P.act(daa[:, :], daa[:, :], AF.Ln, inc=T(s_se))
            P.wait("v", s_se, s_se.n)
            P.ts(dab[:, :], dtt[:, :], 0.0, None, ALU.max)
            P.tt(dtt[:, :], dab[:, :], daa[:, :], ALU.add)
            P.ts(daa[:, :], dtt[:, :], mparam[:, 2:3], None, ALU.mult,
                 inc=T(s_ve))
            cur, oth = daa, dab
            for kst in (1, 2, 4, 8, 16, 32, 64, 128):
                cv = cur[:, :].rearrange("p (c s) -> p c s", c=16)
                ov = oth[:, :].rearrange("p (c s) -> p c s", c=16)
                P.tt(ov[:, :, kst:], cv[:, :, kst:], cv[:, :, 0:CH - kst],
                     ALU.add)
                P.cp(ov[:, :, 0:kst], cv[:, :, 0:kst])
                cur, oth = oth, cur
            if cur is not daa:
                P.cp(daa[:, :], cur[:, :])
            P._op("v", "tensor_copy", small[0:64, 0:1], daa[:, 0:1], inc=T(s_ve))
            v_daa = s_ve.n
            ndl = vA(57344, 64, 16, F32)
            cd = vA(57408, 64, 16, F32)
            P.ts(ndl[:, :], daa[:, :].rearrange("p (c s) -> p c s", c=16)
                 [:, :, 255:256].squeeze(2), 1.0, None, ALU.mult)
            P.wait("s", s_ve, s_ve.n)
            P.act(cd[:, :], ndl[:, :], AF.Exp, inc=T(s_se))
            decay = vA(32768, 64, S, BF16)
            for c in range(16):
                P.act(decay[:, c * CH:(c + 1) * CH],
                      daa[:, c * CH:(c + 1) * CH], AF.Exp, scale=-1.0,
                      bias=ndl[:, c:c + 1],
                      inc=T(s_se) if c == 15 else None)
            v_decay = s_se.n
            dtT = vA(40960, 128, 2048, F32)
            biasC = vA(24576, 128, 2048, F32)
            lnb = vA(57472, 128, 2048, F32)
            dtdecT = vA(36864, 128, 2048, BF16)
            decayT = vA(38912, 128, 2048, BF16)
            P.wait("t", s_ve, v_daa)
            P.wait("t", s_se, v_decay)
            tp_evac = {}
            tp2_evac = {}
            for st in range(32):
                bko = st % 2
                if st >= 2:
                    P.wait("t", s_ve, tp_evac[st - 2])
                P.mm(bank(bko, n=64, p=128), dtt[:, st * 128:(st + 1) * 128],
                     identf[0:64, 0:64], start=True, stop=False)
                P.mm(bank(bko, n=64, off=64, p=128),
                     daa[:, st * 128:(st + 1) * 128],
                     identf[0:64, 0:64], start=False, stop=True,
                     inc=T(s_pe))
                v_t1 = s_pe.n
                P.wait("v", s_pe, v_t1)
                P.cp(dtT[:, st * 64:(st + 1) * 64], bank(bko, n=64, p=128))
                P.ts(biasC[:, st * 64:(st + 1) * 64],
                     bank(bko, n=64, off=64, p=128), -1.0, None, ALU.mult,
                     inc=T(s_ve))
                tp_evac[st] = s_ve.n
            for st in range(32):
                bko = 2 + st % 2
                if st >= 2:
                    P.wait("t", s_ve, tp2_evac[st - 2])
                pbt = PS[:, bko * 512:(bko + 1) * 512].bitcast(BF16)
                P.tp(pbt[:, 0:64], decay[:, st * 128:(st + 1) * 128],
                     identb[0:64, 0:64], inc=T(s_pe))
                P.wait("v", s_pe, s_pe.n)
                P.cp(decayT[:, st * 64:(st + 1) * 64], pbt[:, 0:64],
                     inc=T(s_ve))
                tp2_evac[st] = s_ve.n
            P.wait("s", s_ve, s_ve.n)
            P.act(lnb[:, :], dtT[:, :], AF.Ln, inc=T(s_se))
            P.wait("v", s_se, s_se.n)
            P.tt(biasC[:, :], biasC[:, :], lnb[:, :], ALU.add)
            P.tt(dtdecT[:, :], dtT[:, :], decayT[:, :], ALU.mult,
                 inc=T(s_ve))
            for s2 in range(2):
                if win_stage.vals[s2]:
                    P.wait("y", win_stage.sems[s2], win_stage.vals[s2])
            return s_ve.n

        xs_ring = Ring(P, "xsr", 1, "y")
        y_store = StoreTrack(P, "yst", 4, "g")

        def ssd_phase():
            daa = vA(0, 64, S, F32)
            xssd = vA(8192, 128, 8192, BF16)
            x_tok = [vA(16384, 128, S, BF16), vA(20480, 128, S, BF16)]
            biasC = vA(24576, 128, 2048, F32)
            cdrow = vA(28672, 1, DIN, F32)
            dtdecT = vA(36864, 128, 2048, BF16)
            xdtd = [vA(40960, 128, S, BF16), vA(45056, 128, S, BF16)]
            Ssc = vA(49152, 128, DIN, F32)
            cd = vA(57408, 64, 16, F32)
            ystage = [vA(61568 + i * 256, 128, 256, BF16) for i in range(4)]
            btok = [vA(62592, 128, 128, BF16), vA(62720, 128, 128, BF16)]
            CBm = [vA(62848, 128, 256, BF16), vA(63104, 128, 256, BF16)]
            cdsmall = vA(63872, 1, 64, F32)
            bB = vB(0, 128, S, BF16)
            cB = vB(4096, 128, S, BF16)
            S_bf = vB(8192, 128, DIN, BF16)
            M0b = vB(12288, 128, 4096, BF16)
            M1b = vB(16384, 128, 4096, BF16)
            Ceb = vB(20480, 128, 4096, BF16)
            barrier_loads("y")
            P.dma("y", bB[:, :], bT_d, s_ld)
            P.dma("y", cB[:, :], cT_d, s_ld)
            v_bc = s_ld.n
            P.memset(Ssc[:, :], 0.0, inc=T(s_ve))
            v_prev_yevac = 0
            v_prev_scan = 0
            v_prev_waveB = {}
            for c in range(NCH):
                x_rdy = xs_ring.load([(xssd
                    .rearrange("p (t s) -> p t s", s=CH),
                    xssdT_d.rearrange("(t p) s -> p t s", p=128)
                    [:, :, c * CH:(c + 1) * CH])])
                P.wait("t", *x_rdy)
                if v_prev_scan:
                    P.wait("t", s_ve, v_prev_scan)
                tp_ev = {}
                for grp in range(8):
                    js, hq = grp % 2, grp // 2
                    bko = grp % 2
                    if grp >= 2:
                        P.wait("t", s_ve, tp_ev[grp - 2])
                    pbt = PS[:, bko * 512:(bko + 1) * 512].bitcast(BF16)
                    for k in range(8):
                        idx = grp * 8 + k
                        ht, js2 = idx // 2, idx % 2
                        P.tp(pbt[:, k * 128:(k + 1) * 128],
                             xssd[:, ht * 256 + js2 * 128:
                                  ht * 256 + js2 * 128 + 128],
                             identb[:, :], inc=T(s_pe) if k == 7 else None)
                    P.wait("v", s_pe, s_pe.n)
                    for k in range(8):
                        idx = grp * 8 + k
                        ht, js2 = idx // 2, idx % 2
                        P.cp(x_tok[js2][:, ht * 128:(ht + 1) * 128],
                             pbt[:, k * 128:(k + 1) * 128],
                             inc=T(s_ve) if k == 7 else None)
                    tp_ev[grp] = s_ve.n
                v_xtok = s_ve.n
                for js in range(2):
                    dslice = dtdecT[:, (c * 2 + js) * 64:(c * 2 + js + 1) * 64]
                    P.tt(xdtd[js][:, :].rearrange("p (h u) -> p h u", u=64),
                         x_tok[js][:, :].rearrange("p (h u) -> p h u", u=64),
                         dslice.unsqueeze(2).to_broadcast([128, 64, 64]),
                         ALU.mult, inc=T(s_ve) if js == 1 else None)
                v_xdtd = s_ve.n
                P.wait("t", s_ld, v_bc)
                P.wait("t", s_ve, v_xtok)
                pbt2 = PS[:, 2 * 512:3 * 512].bitcast(BF16)
                for js in range(2):
                    P.tp(pbt2[:, js * 128:(js + 1) * 128],
                         bB[:, c * CH + js * 128:c * CH + (js + 1) * 128],
                         identb[:, :], inc=T(s_pe) if js == 1 else None)
                P.wait("v", s_pe, s_pe.n)
                P.cp(btok[0][:, :], pbt2[:, 0:128])
                P.cp(btok[1][:, :], pbt2[:, 128:256], inc=T(s_ve))
                v_btok = s_ve.n
                for js in range(2):
                    P.mm(bank(6 + js, n=256), bB[:, c * CH + js * 128:
                                                 c * CH + (js + 1) * 128],
                         cB[:, c * CH:(c + 1) * CH], start=True, stop=True,
                         inc=T(s_pe))
                    P.wait("v", s_pe, s_pe.n)
                    P.tt(CBm[js][:, :], bank(6 + js, n=256),
                         cmask[:, js * 256:(js + 1) * 256], ALU.mult,
                         inc=T(s_ve))
                v_cbm = s_ve.n
                if c > 0:
                    P.cp(S_bf[:, :], Ssc[:, :], inc=T(s_ve))
                v_sbf = s_ve.n
                bc_guard = {4: (0, 0), 5: (0, 0), 6: (0, 0), 7: (0, 0)}
                v_waveB_end = 0
                for wave in range(4):
                    if v_waveB_end:
                        P.wait("v", s_pe, v_waveB_end)
                        P.wait("s", s_pe, v_waveB_end)
                    for pr2 in range(8):
                        hh0 = wave * 16 + pr2 * 2
                        bcb = 4 + (pr2 % 4)
                        g_v, g_s = bc_guard[bcb]
                        if g_v:
                            P.wait("t", s_ve, g_v)
                        if g_s:
                            P.wait("t", s_se, g_s)
                        for u in range(2):
                            sel = identf[0:64, hh0 + u:hh0 + u + 1]
                            P.mm(bank(bcb, n=256, off=u * 256),
                                 sel.to_broadcast([64, 128]),
                                 daa[:, c * CH:(c + 1) * CH], start=True,
                                 stop=True, inc=T(s_pe) if u == 1 else None)
                        v_bcst = s_pe.n
                        P.wait("v", s_pe, v_bcst)
                        P.wait("s", s_pe, v_bcst)
                        for u in range(2):
                            hh = hh0 + u
                            hl = pr2 * 2 + u
                            for js in range(2):
                                dst = M0b if js == 0 else M1b
                                P.ts(dst[:, hl * 256:(hl + 1) * 256],
                                     bank(bcb, n=256, off=u * 256),
                                     biasC[:, (2 * c + js) * 64 + hh:
                                           (2 * c + js) * 64 + hh + 1],
                                     30.0, ALU.add, ALU.min,
                                     inc=T(s_ve)
                                     if (u == 1 and js == 1) else None)
                        P.act(Ceb[:, pr2 * 512:(pr2 + 1) * 512],
                              bank(bcb, n=512), AF.Exp, inc=T(s_se))
                        bc_guard[bcb] = (s_ve.n, s_se.n)
                    v_pre = s_ve.n
                    P.wait("s", s_ve, v_pre)
                    P.act(M0b[:, :], M0b[:, :], AF.Exp)
                    P.act(M1b[:, :], M1b[:, :], AF.Exp, inc=T(s_se))
                    v_exp = s_se.n
                    P.wait("v", s_se, v_exp)
                    P.wait("v", s_ld, v_bc)
                    vM0 = M0b[:, :].rearrange("p (h i) -> p h i", h=16)
                    vM1 = M1b[:, :].rearrange("p (h i) -> p h i", h=16)
                    vCe = Ceb[:, :].rearrange("p (h i) -> p h i", h=16)
                    P.tt(vM0, vM0, CBm[0][:, :].unsqueeze(1)
                         .to_broadcast([128, 16, 256]), ALU.mult)
                    P.tt(vM1, vM1, CBm[1][:, :].unsqueeze(1)
                         .to_broadcast([128, 16, 256]), ALU.mult)
                    P.tt(vCe, vCe, cB[:, c * CH:(c + 1) * CH].unsqueeze(1)
                         .to_broadcast([128, 16, 256]), ALU.mult,
                         inc=T(s_ve))
                    v_wmult = s_ve.n
                    for bk2 in range(4):
                        guard = v_prev_waveB.get(bk2, 0)
                        if guard:
                            P.wait("t", s_ve, guard)
                        P.wait("t", s_ve, v_wmult)
                        P.wait("t", s_ve, v_xdtd)
                        for k in range(4):
                            hl = bk2 * 4 + k
                            hh = wave * 16 + hl
                            po = PS[(k % 2) * 64:(k % 2) * 64 + 64,
                                    bk2 * 512 + (k // 2) * 256:
                                    bk2 * 512 + (k // 2) * 256 + 256]
                            st0 = (k < 2)
                            sp1 = (k >= 2)
                            fin = (k == 3)
                            P.mm(po, x_tok[0][:, hh * 64:(hh + 1) * 64],
                                 M0b[:, hl * 256:(hl + 1) * 256], start=st0,
                                 stop=False)
                            P.mm(po, x_tok[1][:, hh * 64:(hh + 1) * 64],
                                 M1b[:, hl * 256:(hl + 1) * 256], start=False,
                                 stop=(c == 0 and sp1),
                                 inc=T(s_pe) if (c == 0 and fin) else None)
                            if c > 0:
                                P.wait("t", s_ve, v_sbf)
                                P.mm(po, S_bf[:, hh * 64:(hh + 1) * 64],
                                     Ceb[:, hl * 256:(hl + 1) * 256],
                                     start=False, stop=sp1,
                                     inc=T(s_pe) if fin else None)
                        v_bmm = s_pe.n
                        v_waveB_end = v_bmm
                        P.wait("v", s_pe, v_bmm)
                        for pr in range(2):
                            t_glob = (wave * 16 + bk2 * 4) // 2 + pr
                            ysl = y_store.guard("v")
                            P.ts(small[:, 0:256], xssd[:, t_glob * 256:
                                                       (t_glob + 1) * 256],
                                 dp_c[:, t_glob:t_glob + 1], None, ALU.mult)
                            P.tt(ystage[ysl][:, :],
                                 PS[:, bk2 * 512 + pr * 256:
                                    bk2 * 512 + pr * 256 + 256],
                                 small[:, 0:256], ALU.add, inc=T(s_ve))
                            v_prev_waveB[bk2] = s_ve.n
                            P.wait("g", s_ve, s_ve.n)
                            y_store.store(yT_d[t_glob * 128:(t_glob + 1) * 128,
                                               c * CH:(c + 1) * CH],
                                          ystage[ysl])
                v_yevac = s_ve.n
                v_prev_yevac = v_yevac
                if c < NCH - 1:
                    P.wait("t", s_se, s_se.n)
                    P.wait("t", s_ve, s_ve.n)
                    P.mm(bank(6, n=64, off=320, p=1), cd[:, c:c + 1],
                         identf[0:64, 0:64], start=True, stop=True,
                         inc=T(s_pe))
                    P.wait("s", s_pe, s_pe.n)
                    P.act(cdsmall[0:1, :], bank(6, n=64, off=320, p=1),
                          AF.Copy, inc=T(s_se))
                    P.wait("v", s_se, s_se.n)
                    P.ts(cdrow[0:1, :].rearrange("p (h u) -> p h u", u=64),
                         cdsmall[0:1, :].unsqueeze(2)
                         .to_broadcast([1, 64, 64]), 1.0, None, ALU.mult,
                         inc=T(s_ve))
                    v_cdrow = s_ve.n
                    P.wait("t", s_ve, v_cdrow)
                    for bq in range(8):
                        P.mm(bank(bq), onesf[:, :],
                             cdrow[0:1, bq * 512:(bq + 1) * 512], start=True,
                             stop=True, inc=T(s_pe) if bq == 7 else None)
                    P.wait("v", s_pe, s_pe.n)
                    P.tt(Ssc[:, :], Ssc[:, :], PS[:, 0:DIN], ALU.mult,
                         inc=T(s_ve))
                    v_smul = s_ve.n
                    P.wait("t", s_ve, v_smul)
                    for js in range(2):
                        P.wait("t", s_ve, v_btok)
                        for hh in range(64):
                            P.mm(PS[:, hh * 64:(hh + 1) * 64], btok[js],
                                 xdtd[js][:, hh * 64:(hh + 1) * 64],
                                 start=(js == 0 and hh % 8 == 0),
                                 stop=(js == 1 and hh % 8 == 7),
                                 inc=T(s_pe)
                                 if (js == 1 and hh == 63) else None)
                    P.wait("v", s_pe, s_pe.n)
                    P.tt(Ssc[:, :], Ssc[:, :], PS[:, 0:DIN], ALU.add,
                         inc=T(s_ve))
                    v_prev_scan = s_ve.n
                    v_prev_waveB = {}
                else:
                    v_prev_scan = v_yevac
                xs_ring.mark((s_ve, v_yevac))
            for s2 in range(4):
                if y_store.vals[s2]:
                    P.wait("y", y_store.sems[s2], y_store.vals[s2])
            return 0

        m4_yz = Ring(P, "m4yz", 1, "y")
        wo2_ring = Ring(P, "wo2", 2, "g")
        m4_rring = Ring(P, "m4r", 4, "y")
        m4_store = StoreTrack(P, "m4s", 4, "g")

        def m4_phase(src_resid, dst_resid):
            barrier_loads("y")
            barrier_loads("g")
            yblk = vA(0, 128, 16384, BF16)
            zblk = vA(16384, 128, 16384, BF16)
            yn = vA(32768, 128, 16384, BF16)
            sgz = vB(0, 128, 512, BF16)
            sq = vB(512, 128, 512, BF16)
            msq = vB(1024, 128, 512, F32)
            rsl_ = [vB(2048 + i * 1024, 128, 512, F32) for i in range(4)]
            wslot2 = [WT[:, 0:4096], WT[:, 4096:8192]]
            m4_evac = {}
            for sb in range(8):
                yz_rdy = m4_yz.load([
                    (yblk.rearrange("p (t s) -> p t s", s=512),
                     yT_d.rearrange("(t p) s -> p t s", p=128)
                     [:, :, sb * 512:(sb + 1) * 512]),
                    (zblk.rearrange("p (t s) -> p t s", s=512),
                     zT_d.rearrange("(t p) s -> p t s", p=128)
                     [:, :, sb * 512:(sb + 1) * 512])])
                P.wait("s", *yz_rdy)
                P.wait("v", *yz_rdy)
                for t in range(32):
                    P.wait("s", s_ve, s_ve.n)
                    P.act(sgz[:, :], zblk[:, t * 512:(t + 1) * 512], AF.Silu,
                          inc=T(s_se))
                    P.wait("v", s_se, s_se.n)
                    P.tt(yn[:, t * 512:(t + 1) * 512],
                         yblk[:, t * 512:(t + 1) * 512], sgz[:, :], ALU.mult)
                    if t >= 1:
                        P.wait("v", s_pe, s_pe.n)
                    P.tt(sq[:, :], yn[:, t * 512:(t + 1) * 512],
                         yn[:, t * 512:(t + 1) * 512], ALU.mult,
                         inc=T(s_ve))
                    P.wait("t", s_ve, s_ve.n)
                    P.mm(bank(7), onesb[:, :], sq[:, :], start=(t == 0),
                         stop=(t == 31), inc=T(s_pe))
                v_st = s_pe.n
                P.wait("v", s_pe, v_st)
                P.ts(msq[:, :], bank(7), 1.0 / DIN, float(EPS), ALU.mult,
                     ALU.add, inc=T(s_ve))
                P.wait("s", s_ve, s_ve.n)
                rw = vB(3072 + 4096, 128, 512, F32)
                P.act(rw[:, :], msq[:, :], AF.Ln)
                P.act(rw[:, :], rw[:, :], AF.Exp, scale=-0.5, inc=T(s_se))
                P.wait("v", s_se, s_se.n)
                rw2 = vB(3072 + 5120, 128, 512, F32)
                P.tt(rw2[:, :], msq[:, :], rw[:, :], ALU.mult)
                P.tt(rw2[:, :], rw2[:, :], rw[:, :], ALU.mult)
                P.ts(rw2[:, :], rw2[:, :], -0.5, 1.5, ALU.mult, ALU.add)
                P.tt(msq[:, :], rw[:, :], rw2[:, :], ALU.mult, inc=T(s_ve))
                P.wait("v", s_ve, s_ve.n)
                for t in range(32):
                    P.ts(yn[:, t * 512:(t + 1) * 512],
                         yn[:, t * 512:(t + 1) * 512],
                         normw_c[:, t:t + 1], None, ALU.mult)
                    P.tt(yn[:, t * 512:(t + 1) * 512],
                         yn[:, t * 512:(t + 1) * 512], msq[:, :], ALU.mult,
                         inc=T(s_ve) if t == 31 else None)
                v_yn = s_ve.n
                m4_yz.mark((s_ve, v_yn))
                for dmt in range(16):
                    sl = wo2_ring.nload % 2
                    w_rdy = wo2_ring.load([(wslot2[sl]
                        .rearrange("p (c i) -> p c i", c=32),
                        woutT.rearrange("(c p) i -> p c i", p=128)
                        [:, :, dmt * 128:(dmt + 1) * 128])])
                    bk2 = (dmt % 4) + 4 * ((dmt // 4) % 2)
                    if dmt >= 8:
                        P.wait("t", s_ve, m4_evac[dmt - 8])
                    msl = m4_store.guard("y")
                    r_rdy = m4_rring.load([(rsl_[m4_rring.nload % 4],
                        src_resid[dmt * 128:(dmt + 1) * 128,
                                  sb * 512:(sb + 1) * 512])])
                    rsl = rsl_[(m4_rring.nload - 1) % 4]
                    P.wait("t", *w_rdy)
                    P.wait("t", s_ve, v_yn)
                    for hp in range(32):
                        P.mm(bank(bk2), wslot2[sl][:, hp * 128:(hp + 1) * 128],
                             yn[:, hp * 512:(hp + 1) * 512],
                             start=(hp == 0), stop=(hp == 31),
                             inc=T(s_pe) if hp == 31 else None)
                    v_mm = s_pe.n
                    wo2_ring.mark((s_pe, v_mm))
                    P.wait("v", s_pe, v_mm)
                    P.wait("v", *r_rdy)
                    P.tt(rsl, rsl, bank(bk2), ALU.add, inc=T(s_ve))
                    m4_evac[dmt] = s_ve.n
                    m4_rring.mark((s_ve, s_ve.n))
                    P.wait("g", s_ve, s_ve.n)
                    m4_store.store(dst_resid[dmt * 128:(dmt + 1) * 128,
                                             sb * 512:(sb + 1) * 512], rsl)
            for s2 in range(4):
                if m4_store.vals[s2]:
                    P.wait("y", m4_store.sems[s2], m4_store.vals[s2])
            return [(m4_store.sems[s2], m4_store.vals[s2])
                    for s2 in range(4) if m4_store.vals[s2]]

        v_xn1 = ln_phase(x_in, 0)
        if PHASES >= 2:
            qkv_phase(v_xn1)
        if PHASES >= 3:
            flash_phase()
        if PHASES >= 4:
            wo_stores = wo_phase(x_in, xres)
        if PHASES >= 5:
            ffn_phase(xres, xres, wo_stores)
        if PHASES >= 7:
            v_xn3 = ln_phase(xres, 0)
            win_phase(v_xn3)
        if PHASES >= 8:
            ssd_phase()
        if PHASES >= 9:
            m4_stores = m4_phase(xres, xres)
        if PHASES >= 10:
            ffn_phase(xres, out_xT, m4_stores)

        if PHASES >= 10:
            tail_skip = True
        barrier_loads("y")
        if PHASES < 10:
            tmpo = vB(0, 128, S, F32)
            tail = StoreTrack(P, "tail", 1, "g")
            for t in range(16):
                P.dma("y", tmpo, xres[t * 128:(t + 1) * 128, :], s_ld)
                P.wait("g", s_ld, s_ld.n)
                tail.store(out_xT[t * 128:(t + 1) * 128, :], tmpo)
                P.wait("y", tail.sems[0], tail.vals[0])
            P.wait("y", tail.sems[0], tail.vals[0])

        blk = stack.enter_context(nc.Block())

        @blk.tensor
        def _(e):
            for f in P.q["t"]:
                f(e)

        @blk.vector
        def _(e):
            for f in P.q["v"]:
                f(e)

        @blk.scalar
        def _(e):
            for f in P.q["s"]:
                f(e)

        @blk.gpsimd
        def _(e):
            for f in P.q["g"]:
                f(e)

        @blk.sync
        def _(e):
            for f in P.q["y"]:
                f(e)

    return nc



def _prep_shared(inputs):
    f32 = np.float32
    bf16 = ml_dtypes.bfloat16
    d = {}
    d["wqT"] = np.ascontiguousarray(np.asarray(inputs["L0_attn_a_Wq"]).T).astype(f32)
    d["wkT"] = np.ascontiguousarray(np.asarray(inputs["L0_attn_a_Wk"]).T).astype(f32)
    d["wvT"] = np.ascontiguousarray(np.asarray(inputs["L0_attn_a_Wv"]).T).astype(f32)
    d["woT"] = np.ascontiguousarray(np.asarray(inputs["L0_attn_a_Wo"]).T).astype(f32)
    d["b_qkvo"] = np.ascontiguousarray(np.stack(
        [np.asarray(inputs["L0_attn_a_bq"]), np.asarray(inputs["L0_attn_a_bk"]),
         np.asarray(inputs["L0_attn_a_bv"]), np.asarray(inputs["L0_attn_a_bo"])])
        .astype(f32).reshape(4, 16, 128, 1))
    invf = 10000.0 ** (-np.arange(0, 128, 2, dtype=np.float64) / 128.0)
    t = np.arange(S, dtype=np.float64)
    fr = np.outer(t, invf)
    cos = np.cos(fr).T.astype(f32)
    sin = np.sin(fr).T.astype(f32)
    d["ropeqc"] = np.ascontiguousarray(
        np.concatenate([cos, cos], 0) * f32(SM_SCALE)).astype(bf16)
    d["ropeqs"] = np.ascontiguousarray(
        np.concatenate([-sin, sin], 0) * f32(SM_SCALE)).astype(bf16)
    d["ropekc"] = np.ascontiguousarray(np.concatenate([cos, cos], 0)).astype(bf16)
    d["ropeks"] = np.ascontiguousarray(np.concatenate([-sin, sin], 0)).astype(bf16)
    swp = np.zeros((128, 128), dtype=f32)
    for i in range(128):
        swp[i, (i + 64) % 128] = 1.0
    d["swapm"] = swp.astype(bf16)
    d["winT"] = np.ascontiguousarray(np.asarray(inputs["L0_attn_b_W_in"]).T).astype(f32)
    d["convw"] = np.ascontiguousarray(
        np.asarray(inputs["L0_attn_b_conv_w"]).astype(f32).reshape(34, 128, 4))
    d["convb"] = np.ascontiguousarray(
        np.asarray(inputs["L0_attn_b_conv_b"]).astype(f32).reshape(34, 128, 1))
    d["dtbias"] = np.asarray(inputs["L0_attn_b_dt_bias"]).astype(f32).reshape(64, 1)
    d["alog"] = np.asarray(inputs["L0_attn_b_A_log"]).astype(f32).reshape(64, 1)
    d["dp"] = np.ascontiguousarray(
        np.repeat(np.asarray(inputs["L0_attn_b_Dp"]).astype(f32), 64)
        .reshape(32, 128, 1))
    d["normw"] = np.ascontiguousarray(
        np.asarray(inputs["L0_attn_b_norm_w"]).astype(f32).reshape(32, 128, 1))
    d["woutT"] = np.ascontiguousarray(np.asarray(inputs["L0_attn_b_W_out"]).T).astype(f32)
    g8 = np.asarray(inputs["L0_ffn_gate_w"]).astype(np.uint8)
    d["g8"] = np.ascontiguousarray(
        g8.reshape(64, 128, 16, 128).transpose(0, 3, 2, 1)).reshape(64, 128, 2048)
    u8 = np.asarray(inputs["L0_ffn_up_w"]).astype(np.uint8)
    d["u8"] = np.ascontiguousarray(
        u8.reshape(64, 128, 16, 128).transpose(0, 3, 2, 1)).reshape(64, 128, 2048)
    dn = np.asarray(inputs["L0_ffn_down_w"]).astype(np.uint8)
    d["dn8"] = np.ascontiguousarray(
        dn.reshape(16, 128, 64, 128).transpose(0, 3, 2, 1)).reshape(16, 128, 8192)
    d["gsc"] = np.asarray(inputs["L0_ffn_gate_scale"]).astype(f32).reshape(64, 128, 1)
    d["usc"] = np.asarray(inputs["L0_ffn_up_scale"]).astype(f32).reshape(64, 128, 1)
    d["dsc"] = np.asarray(inputs["L0_ffn_down_scale"]).astype(f32).reshape(16, 128, 1)
    d["identb"] = np.eye(128, dtype=f32).astype(bf16)
    d["identf"] = np.eye(128, dtype=f32)
    d["onesf"] = np.ones((1, 128), dtype=f32)
    kk = np.arange(128)[:, None]
    qq = np.arange(512)[None, :]
    d["fmask"] = np.ascontiguousarray(np.stack(
        [(qq >= kk + r * 128) for r in range(4)]).astype(f32)).astype(bf16)
    jj = np.arange(128)[:, None]
    ii = np.arange(256)[None, :]
    d["cmask"] = np.ascontiguousarray(np.stack(
        [(ii >= jj + r * 128) for r in range(2)]).astype(f32)).astype(bf16)
    return d


def kernel(**inputs):
    xs = np.asarray(inputs["L0_attn_a_hidden_states"], dtype=np.float32)
    shared = _prep_shared(inputs)
    nc = build_nc()
    in_maps = []
    for b in range(8):
        m = dict(shared)
        m["x_in"] = np.ascontiguousarray(xs[b].T)
        in_maps.append(m)
    res = run_bass_kernel_spmd(nc, in_maps, core_ids=list(range(8)))
    out = np.stack([np.asarray(res.results[b]["out_xT"], dtype=np.float32).T
                    for b in range(8)])
    return out


# version 07
# speedup vs baseline: 1.0520x; 1.0520x over previous
import contextlib
import math
import numpy as np
import ml_dtypes
import concourse.bass as bass
import concourse.mybir as mybir
from concourse.bass_utils import run_bass_kernel_spmd

F32 = mybir.dt.float32
BF16 = mybir.dt.bfloat16
I32 = mybir.dt.int32
U8 = mybir.dt.uint8
AF = mybir.ActivationFunctionType
ALU = mybir.AluOpType
AX = mybir.AxisListType

S = 4096
DM = 2048
NHEAD = 16
DFF = 8192
DIN = 4096
MH = 64
CH = 256
NCH = S // CH
DPROJ = 8512
SM_SCALE = 1.0 / math.sqrt(128.0)
EPS = 1e-5

DEBUG = False
PHASES = 99


def T(sem, k=1):
    sem.tick(k)
    return (sem, k)


class Sem:
    __slots__ = ("h", "n")

    def __init__(self, h):
        self.h = h
        self.n = 0

    def tick(self, k=1):
        self.n += k
        return self.n


class Prog:
    def __init__(self, nc, stack):
        self.nc = nc
        self.stack = stack
        self.q = {e: [] for e in ("t", "v", "s", "g", "y")}
        self.lastwait = {e: {} for e in ("t", "v", "s", "g", "y")}
        self.serial = {}

    def sem(self, name):
        return Sem(self.stack.enter_context(self.nc.semaphore(name)))

    def emit(self, eng, fn):
        self.q[eng].append(fn)

    def wait(self, eng, sem, val):
        if val <= 0:
            return
        lw = self.lastwait[eng]
        if lw.get(id(sem), -1) >= val:
            return
        lw[id(sem)] = val
        self.emit(eng, lambda e, _h=sem.h, _v=val: e.wait_ge(_h, _v))

    def _op(self, eng, meth, *args, inc=None, **kw):
        if inc is None:
            incs = []
        elif isinstance(inc, list):
            incs = list(inc)
        else:
            incs = [inc]
        ser = self.serial.get(eng)
        if ser is not None and meth != "dma_start":
            if eng != "t":
                has = any(s is ser for s, _ in incs)
                if not has:
                    ser.tick()
                    incs.append((ser, 1))
        def f(e, m=meth, a=args, k=kw, ic=tuple(incs)):
            i = getattr(e, m)(*a, **k)
            for sem, amt in ic:
                i.then_inc(sem.h, amt)
        self.emit(eng, f)

    def mm(self, out, lhsT, rhs, start, stop, inc=None):
        self._op("t", "matmul", out, lhsT, rhs, start=start, stop=stop, inc=inc)

    def tp(self, out, in_, ident, inc=None):
        self._op("t", "transpose", out, in_, ident, inc=inc)

    def nop_t(self, inc):
        self._op("t", "nop", inc=inc)

    def cp(self, out, in_, inc=None, eng="v"):
        self._op(eng, "tensor_copy", out, in_, inc=inc)

    def tt(self, out, in0, in1, op, inc=None, eng="v"):
        self._op(eng, "tensor_tensor", out, in0, in1, op, inc=inc)

    def ts(self, out, in0, s1, s2=None, op0=ALU.mult, op1=None, inc=None):
        if op1 is None:
            self._op("v", "tensor_scalar", out, in0, s1, None, op0, inc=inc)
        else:
            self._op("v", "tensor_scalar", out, in0, s1, s2, op0, op1, inc=inc)

    def red(self, out, in_, op, inc=None):
        self._op("v", "tensor_reduce", out, in_, AX.X, op, inc=inc)

    def recip(self, out, in_, inc=None):
        self._op("v", "reciprocal", out, in_, inc=inc)

    def memset(self, ap, val, inc=None, eng="v"):
        self._op(eng, "memset", ap, val, inc=inc)

    def ata(self, out, in0, in1, scale, bias, inc=None):
        self._op("v", "affine_then_add", out, in0, in1, scale, bias, inc=inc)

    def act(self, out, in_, func, bias=None, scale=None, accum_out=None, inc=None):
        kw = {}
        if bias is not None:
            kw["bias"] = bias
        if scale is not None:
            kw["scale"] = scale
        if accum_out is not None:
            kw["accum_out"] = accum_out
        self._op("s", "activation", out, in_, func, inc=inc, **kw)

    def dma(self, queue, out, in_, sem, slow=False, accum=False):
        kw = {}
        if slow:
            kw["allow_slow_non_contiguous"] = True
        if accum:
            kw["accum_op"] = ALU.add
        self._op(queue, "dma_start", out, in_, inc=(sem, 16), **kw)
        return sem.tick(16)


class Ring:

    def __init__(self, P, name, nslots, queue):
        self.P = P
        self.queue = queue
        self.sems = [P.sem(f"{name}{i}") for i in range(nslots)]
        self.consumed = [None] * nslots
        self.nslots = nslots
        self.nload = 0
        self.mark_i = 0

    def load(self, pairs, slow=False, fill=None):
        s = self.nload % self.nslots
        if self.consumed[s] is not None:
            self.P.wait(self.queue, *self.consumed[s])
        v = 0
        for dst, srcin in pairs:
            if fill is not None:
                self.P._op(self.queue, "memset", dst, fill)
            v = self.P.dma(self.queue, dst, srcin, self.sems[s], slow=slow,
                           accum=(fill is not None))
        self.nload += 1
        return (self.sems[s], v)

    def mark(self, semval):
        self.consumed[self.mark_i % self.nslots] = semval
        self.mark_i += 1


class StoreTrack:

    def __init__(self, P, name, nslots, queue="g"):
        self.P = P
        self.queue = queue
        self.sems = [P.sem(f"{name}{i}") for i in range(nslots)]
        self.vals = [0] * nslots
        self.n = 0
        self.gi = 0

    def guard(self, eng):
        s = self.gi % len(self.sems)
        self.gi += 1
        if self.vals[s]:
            self.P.wait(eng, self.sems[s], self.vals[s])
        return s

    def store(self, dst, srcin):
        s = self.n % len(self.sems)
        self.vals[s] = self.P.dma(self.queue, dst, srcin, self.sems[s])
        self.n += 1


def build_nc():
    nc = bass.Bass("TRN2", target_bir_lowering=False)
    stack = contextlib.ExitStack()
    skind = "ExternalOutput" if DEBUG else "Internal"

    def din(name, shape, dt):
        return nc.dram_tensor(name, shape, dt, kind="ExternalInput").ap()

    x_in = din("x_in", [DM, S], F32)
    wqT = din("wqT", [DM, DM], F32)
    wkT = din("wkT", [DM, DM], F32)
    wvT = din("wvT", [DM, DM], F32)
    woT = din("woT", [DM, DM], F32)
    b_qkvo = din("b_qkvo", [4, 16, 128, 1], F32)
    ropeqc_d = din("ropeqc", [128, S], BF16)
    ropeqs_d = din("ropeqs", [128, S], BF16)
    ropekc_d = din("ropekc", [128, S], BF16)
    ropeks_d = din("ropeks", [128, S], BF16)
    swapm_d = din("swapm", [128, 128], BF16)
    winT = din("winT", [DM, DPROJ], F32)
    convw_d = din("convw", [34, 128, 4], F32)
    convb_d = din("convb", [34, 128, 1], F32)
    dtbias_d = din("dtbias", [64, 1], F32)
    alog_d = din("alog", [64, 1], F32)
    dp_d = din("dp", [32, 128, 1], F32)
    normw_d = din("normw", [32, 128, 1], F32)
    woutT = din("woutT", [DIN, DM], F32)
    g8_d = din("g8", [64, 128, 2048], U8)
    u8_d = din("u8", [64, 128, 2048], U8)
    dn8_d = din("dn8", [16, 128, 8192], U8)
    gsc_d = din("gsc", [64, 128, 1], F32)
    usc_d = din("usc", [64, 128, 1], F32)
    dsc_d = din("dsc", [16, 128, 1], F32)
    identb_d = din("identb", [128, 128], BF16)
    identf_d = din("identf", [128, 128], F32)
    onesf_d = din("onesf", [1, 128], F32)
    fmask_d = din("fmask", [4, 128, 512], BF16)
    cmask_d = din("cmask", [2, 128, 256], BF16)

    out_xT = nc.dram_tensor("out_xT", [DM, S], F32, kind="ExternalOutput").ap()

    def dscr(name, shape, dt):
        return nc.dram_tensor(name, shape, dt, kind=skind).ap()

    xres = dscr("xres", [DM, S], F32)
    qT_d = dscr("qT", [DM, S], BF16)
    kT_d = dscr("kT", [DM, S], BF16)
    vtok_d = dscr("vtok", [NHEAD, 32, 128, 128], BF16)
    ctx_d = dscr("ctx", [DM, S], BF16)
    zT_d = dscr("zT", [DIN, S], BF16)
    xssdT_d = dscr("xssdT", [DIN, S], BF16)
    bT_d = dscr("bT", [128, S], BF16)
    cT_d = dscr("cT", [128, S], BF16)
    yT_d = dscr("yT", [DIN, S], BF16)

    with stack:
        def sbuf(name, shape, dt):
            return stack.enter_context(nc.sbuf_tensor(name, shape, dt))

        A = sbuf("A", [128, 65536], BF16)
        B = sbuf("Bb", [128, 24576], BF16)
        WT = sbuf("WT", [128, 8192], BF16)
        identb = sbuf("identb_s", [128, 128], BF16)
        identf = sbuf("identf_s", [128, 128], F32)
        onesf = sbuf("onesf_s", [1, 128], F32)
        onesb = sbuf("onesb", [128, 128], BF16)
        fmask = sbuf("fmask_s", [128, 4 * 512], BF16)
        cmask = sbuf("cmask_s", [128, 2 * 256], BF16)
        biasq = sbuf("biasq", [128, 4 * 16], F32)
        scales = sbuf("scales", [128, 144], F32)
        convw = sbuf("convw_s", [128, 34 * 4], F32)
        convb = sbuf("convb_s", [128, 34], F32)
        mparam = sbuf("mparam", [64, 8], F32)
        dp_c = sbuf("dp_c", [128, 32], F32)
        normw_c = sbuf("normw_c", [128, 32], F32)
        small = sbuf("small", [128, 512], F32)
        swapm = sbuf("swapm_s", [128, 128], BF16)

        PS = stack.enter_context(nc.psum_tensor("PS", [128, 4096], F32))

        def bank(b, n=512, p=128, off=0):
            return PS[0:p, b * 512 + off: b * 512 + off + n]

        def vA(c0, p, n, dt):
            if dt == BF16:
                return A[0:p, c0:c0 + n]
            return A[0:p, c0:c0 + 2 * n].bitcast(F32)

        def vB(c0, p, n, dt):
            if dt == BF16:
                return B[0:p, c0:c0 + n]
            return B[0:p, c0:c0 + 2 * n].bitcast(F32)

        P = Prog(nc, stack)
        s_ld = P.sem("s_ld")
        s_cst = P.sem("s_cst")
        s_st = P.sem("s_st")
        s_pe = P.sem("s_pe")
        s_ve = P.sem("s_ve")
        s_se = P.sem("s_se")
        s_wfree = P.sem("s_wfree")
        s_ge = P.sem("s_ge")
        P.serial = {"v": s_ve, "s": s_se, "g": s_ge, "t": s_pe}

        def barrier_loads(queue="y"):
            P.wait(queue, s_pe, s_pe.n)
            P.wait(queue, s_ve, s_ve.n)
            P.wait(queue, s_se, s_se.n)

        P.dma("y", identb[:, :], identb_d, s_ld)
        P.dma("y", swapm[:, :], swapm_d, s_ld)
        P.dma("y", identf[:, :], identf_d, s_ld)
        P.dma("y", onesf[:, :], onesf_d, s_ld)
        P.dma("y", fmask[:, :].rearrange("p (r m) -> p r m", r=4),
              fmask_d.rearrange("r p m -> p r m"), s_ld)
        P.dma("y", cmask[:, :].rearrange("p (r m) -> p r m", r=2),
              cmask_d.rearrange("r p m -> p r m"), s_ld)
        P.dma("y", biasq[:, :].unsqueeze(2),
              b_qkvo.rearrange("w t p u -> p (w t) u"), s_ld, slow=True)
        P.dma("y", scales[:, 0:64].unsqueeze(2),
              gsc_d.rearrange("t p u -> p t u"), s_ld, slow=True)
        P.dma("y", scales[:, 64:128].unsqueeze(2),
              usc_d.rearrange("t p u -> p t u"), s_ld, slow=True)
        P.dma("y", scales[:, 128:144].unsqueeze(2),
              dsc_d.rearrange("t p u -> p t u"), s_ld, slow=True)
        P.dma("y", convw[:, :].rearrange("p (t w) -> p t w", t=34),
              convw_d.rearrange("t p w -> p t w"), s_ld)
        P.dma("y", convb[:, :].unsqueeze(2),
              convb_d.rearrange("t p u -> p t u"), s_ld, slow=True)
        P.dma("y", mparam[:, 0:1], dtbias_d, s_ld)
        P.dma("y", mparam[:, 1:2], alog_d, s_ld)
        P.dma("y", dp_c[:, :].unsqueeze(2),
              dp_d.rearrange("t p u -> p t u"), s_ld, slow=True)
        v_consts = P.dma("y", normw_c[:, :].unsqueeze(2),
                         normw_d.rearrange("t p u -> p t u"), s_ld, slow=True)

        P.wait("v", s_ld, v_consts)
        P.memset(onesb[:, :], 1.0, inc=T(s_ve))
        P.wait("s", s_ld, v_consts)
        P.act(mparam[:, 2:3], mparam[:, 1:2], AF.Exp, inc=T(s_se))
        P.wait("v", s_se, s_se.n)
        P.ts(mparam[:, 2:3], mparam[:, 2:3], -1.0, inc=T(s_ve))

        ln_ring = Ring(P, "lnst", 2, "y")

        def ln_phase(src_d, store_ready):
            xn = [A[:, t * S:(t + 1) * S] for t in range(16)]
            stg = [vB(0, 128, 2048, F32), vB(4096, 128, 2048, F32)]
            sq = vB(8192, 128, S, BF16)
            mean_bf = vB(12288, 128, S, BF16)
            var_bf = vB(16384, 128, S, BF16)
            m2 = vB(20480, 128, S, BF16)
            barrier_loads("y")
            if store_ready:
                P.wait("y", s_st, store_ready)
            cp_done = {}
            for t in range(16):
                for h in range(2):
                    sl = ln_ring.nload % 2
                    rdy = ln_ring.load([(stg[sl],
                        src_d[t * 128:(t + 1) * 128, h * 2048:(h + 1) * 2048])])
                    P.wait("v", *rdy)
                    P.cp(xn[t][:, h * 2048:(h + 1) * 2048], stg[sl],
                         inc=T(s_ve))
                    cp_done[2 * t + h] = s_ve.n
                    ln_ring.mark((s_ve, s_ve.n))
            for t in range(16):
                P.wait("t", s_ve, cp_done[2 * t + 1])
                for sc in range(8):
                    P.mm(bank(sc), onesb[:, :], xn[t][:, sc * 512:(sc + 1) * 512],
                         start=(t == 0), stop=(t == 15),
                         inc=T(s_pe) if (t == 15 and sc == 7) else None)
            v_sums = s_pe.n
            P.wait("v", s_pe, v_sums)
            for sc in range(8):
                P.ts(mean_bf[:, sc * 512:(sc + 1) * 512], bank(sc), 1.0 / DM,
                     inc=T(s_ve) if sc == 7 else None)
            v_mean = s_ve.n
            sq_mm = {}
            for t in range(16):
                if t >= 1:
                    P.wait("v", s_pe, sq_mm[t - 1])
                P.tt(sq[:, :], xn[t], xn[t], ALU.mult, inc=T(s_ve))
                P.wait("t", s_ve, s_ve.n)
                if t == 0:
                    P.wait("t", s_ve, v_mean)
                for sc in range(8):
                    P.mm(bank(sc), onesb[:, :], sq[:, sc * 512:(sc + 1) * 512],
                         start=(t == 0), stop=(t == 15),
                         inc=T(s_pe) if sc == 7 else None)
                sq_mm[t] = s_pe.n
            v_sq = s_pe.n
            P.wait("v", s_pe, v_sq)
            P.tt(m2[:, :], mean_bf[:, :], mean_bf[:, :], ALU.mult)
            for sc in range(8):
                P.ts(var_bf[:, sc * 512:(sc + 1) * 512], bank(sc), 1.0 / DM)
            P.tt(var_bf[:, :], var_bf[:, :], m2[:, :], ALU.subtract)
            P.ts(var_bf[:, :], var_bf[:, :], float(EPS), None, ALU.add,
                 inc=T(s_ve))
            v_var = s_ve.n
            P.wait("s", s_ve, v_var)
            varf = vB(0, 128, S, F32)
            tnewt = vB(8192, 128, 2048, F32)
            P.act(var_bf[:, :], var_bf[:, :], AF.Ln)
            P.act(var_bf[:, :], var_bf[:, :], AF.Exp, scale=-0.5,
                  inc=T(s_se))
            v_rstd0 = s_se.n
            P.wait("v", s_se, v_rstd0)
            for sc in range(8):
                P.ts(varf[:, sc * 512:(sc + 1) * 512], bank(sc), 1.0 / DM)
            P.tt(varf[:, :], varf[:, :], m2[:, :], ALU.subtract)
            P.ts(varf[:, :], varf[:, :], float(EPS), None, ALU.add)
            for sh2 in range(2):
                h0, h1 = sh2 * 2048, (sh2 + 1) * 2048
                P.tt(tnewt[:, :], varf[:, h0:h1], var_bf[:, h0:h1], ALU.mult)
                P.tt(tnewt[:, :], tnewt[:, :], var_bf[:, h0:h1], ALU.mult)
                P.ts(tnewt[:, :], tnewt[:, :], -0.5, 1.5, ALU.mult, ALU.add)
                P.tt(var_bf[:, h0:h1], var_bf[:, h0:h1], tnewt[:, :],
                     ALU.mult, inc=T(s_ve) if sh2 == 1 else None)
            v_rstd = s_ve.n
            P.wait("v", s_ve, v_rstd)
            for t in range(16):
                P.tt(xn[t], xn[t], mean_bf[:, :], ALU.subtract)
                P.tt(xn[t], xn[t], var_bf[:, :], ALU.mult,
                     inc=T(s_ve))
            return s_ve.n

        wt_ring = Ring(P, "wtw", 2, "g")
        qk_store = StoreTrack(P, "qks", 2, "g")

        def qkv_phase(v_xn):
            ropeB = vB(6144, 128, 4 * S, BF16)
            P.wait("y", s_ve, v_xn)
            P.dma("y", ropeB[:, 0:S], ropeqc_d, s_ld)
            P.dma("y", ropeB[:, S:2 * S], ropeqs_d, s_ld)
            P.dma("y", ropeB[:, 2 * S:3 * S], ropekc_d, s_ld)
            P.dma("y", ropeB[:, 3 * S:4 * S], ropeks_d, s_ld)
            v_rope_ld = s_ld.n
            xn = [A[:, t * S:(t + 1) * S] for t in range(16)]
            tmp = vB(0, 128, 2048, BF16)
            stage = [vB(2048, 128, 2048, BF16), vB(4096, 128, 2048, BF16)]
            sw_sb = vB(22528, 128, 2048, BF16)
            wslot = [WT[:, 0:2048], WT[:, 2048:4096]]
            evac_done = {}
            gidx = 0
            for wdram, odram, bi, isqk in ((wqT, qT_d, 0, True),
                                           (wkT, kT_d, 1, True),
                                           (wvT, vtok_d, 2, False)):
                for it in range(16):
                    sl = wt_ring.nload % 2
                    w_rdy = wt_ring.load([(wslot[sl]
                        .rearrange("p (c i) -> p c i", c=16),
                        wdram.rearrange("(c p) i -> p c i", p=128)
                        [:, :, it * 128:(it + 1) * 128])])
                    P.wait("t", *w_rdy)
                    P.wait("t", s_ve, v_xn)
                    for sh in range(2):
                        g = gidx
                        gidx += 1
                        bk = (g % 2) * 4
                        if g >= 2:
                            P.wait("t", s_ve, evac_done[g - 2])
                        for d in range(16):
                            lw = wslot[sl][:, d * 128:(d + 1) * 128]
                            for sc in range(4):
                                P.mm(bank(bk + sc), lw,
                                     xn[d][:, sh * 2048 + sc * 512:
                                           sh * 2048 + (sc + 1) * 512],
                                     start=(d == 0), stop=(d == 15),
                                     inc=T(s_pe)
                                     if (d == 15 and sc == 3) else None)
                        v_mm = s_pe.n
                        if sh == 1:
                            wt_ring.mark((s_pe, v_mm))
                        P.wait("v", s_pe, v_mm)
                        psg = PS[:, bk * 512:(bk + 4) * 512]
                        ssl = qk_store.guard("v")
                        qs = stage[ssl]
                        if isqk:
                            P.wait("v", s_ld, v_rope_ld)
                            P.ts(tmp[:, :], psg, biasq[:, bi * 16 + it:
                                                       bi * 16 + it + 1],
                                 None, ALU.add, inc=T(s_ve))
                            v_tmp = s_ve.n
                            P.wait("t", s_ve, v_tmp)
                            for sc in range(4):
                                P.mm(bank(bk + sc), swapm[:, :],
                                     tmp[:, sc * 512:(sc + 1) * 512],
                                     start=True, stop=True,
                                     inc=T(s_pe) if sc == 3 else None)
                            v_sw = s_pe.n
                            co = bi * 2 * S + sh * 2048
                            cosA = ropeB[:, co:co + 2048]
                            sinA = ropeB[:, co + S:co + S + 2048]
                            P.tt(qs[:, :], tmp[:, :], cosA, ALU.mult)
                            P.wait("v", s_pe, v_sw)
                            P.tt(sw_sb[:, :], psg, sinA, ALU.mult)
                            P.tt(qs[:, :], qs[:, :], sw_sb[:, :], ALU.add,
                                 inc=T(s_ve))
                            evac_done[g] = s_ve.n
                            P.wait("g", s_ve, s_ve.n)
                            qk_store.store(odram[it * 128:(it + 1) * 128,
                                                 sh * 2048:(sh + 1) * 2048], qs)
                        else:
                            P.ts(tmp[:, :], psg, biasq[:, 32 + it:32 + it + 1],
                                 None, ALU.add, inc=T(s_ve))
                            v_bias = s_ve.n
                            P.wait("t", s_ve, v_bias)
                            pbt = PS[:, bk * 512:(bk + 2) * 512].bitcast(BF16)
                            for k in range(16):
                                P.tp(pbt[:, k * 128:(k + 1) * 128],
                                     tmp[:, k * 128:(k + 1) * 128], identb[:, :],
                                     inc=T(s_pe) if k == 15 else None)
                            v_tp = s_pe.n
                            P.wait("v", s_pe, v_tp)
                            P.cp(qs[:, :], pbt[:, 0:2048], inc=T(s_ve))
                            evac_done[g] = s_ve.n
                            P.wait("g", s_ve, s_ve.n)
                            qk_store.store(vtok_d[it, sh * 16:(sh + 1) * 16]
                                           .rearrange("c p m -> p c m"),
                                           qs[:, :].rearrange("p (c m) -> p c m",
                                                              c=16))
            v_all = s_st.n
            for s2 in range(2):
                if qk_store.vals[s2]:
                    P.wait("y", qk_store.sems[s2], qk_store.vals[s2])
            return 0

        fk_ring = Ring(P, "fkqv", 2, "y")
        fc_store = StoreTrack(P, "fctx", 4, "g")

        def flash_phase():
            scores = vA(0, 128, 16384, F32)
            expS = vA(32768, 128, 16384, BF16)
            maxacc = vA(49152, 128, 512, F32)
            perq = vA(50176, 128, 1, F32)
            negm = vA(50180, 128, 1, F32)
            m11 = vA(50184, 1, 8, F32)
            rrow = vA(50208, 1, 512, F32)
            rb_sb = vA(51232, 128, 512, F32)
            cstage = [vA(52256 + i * 512, 128, 512, BF16) for i in range(4)]
            kqv = [dict(K=vB(sl * 12288, 128, S, BF16),
                        Q=vB(sl * 12288 + 4096, 128, S, BF16),
                        V=vB(sl * 12288 + 8192, 128, S, BF16)) for sl in range(2)]
            barrier_loads("y")
            hload = {}

            def load_head(hh):
                sl2 = fk_ring.nload % 2
                hload[hh] = fk_ring.load([
                    (kqv[sl2]["K"], kT_d[hh * 128:(hh + 1) * 128, :]),
                    (kqv[sl2]["Q"], qT_d[hh * 128:(hh + 1) * 128, :]),
                    (kqv[sl2]["V"], vtok_d[hh].rearrange("c p m -> p c m"))])

            load_head(0)
            load_head(1)
            qk_copy = {}
            v_ctx_read = 0
            prev_maxdone = 0
            prev_exp_reads = 0
            prev_pv = 0
            it_idx = 0
            for h in range(NHEAD):
                sl = h % 2
                K_, Q_, V_ = kqv[sl]["K"], kqv[sl]["Q"], kqv[sl]["V"]
                P.wait("t", *hload[h])
                for qb in range(8):
                    nck = 4 * (qb + 1)
                    P.wait("s", s_ve, prev_maxdone)
                    for kc in range(nck):
                        bk = kc % 4
                        if bk in qk_copy:
                            P.wait("t", s_se, qk_copy[bk])
                        P.mm(bank(bk), K_[:, kc * 128:(kc + 1) * 128],
                             Q_[:, qb * 512:(qb + 1) * 512], start=True, stop=True,
                             inc=T(s_pe))
                        P.wait("s", s_pe, s_pe.n)
                        P.act(scores[:, kc * 512:(kc + 1) * 512], bank(bk), AF.Copy,
                              inc=T(s_se))
                        qk_copy[bk] = s_se.n
                        P.wait("v", s_se, s_se.n)
                        if kc == 0:
                            P.ts(maxacc[:, :], scores[:, 0:512], 1.0)
                        else:
                            P.tt(maxacc[:, :], maxacc[:, :],
                                 scores[:, kc * 512:(kc + 1) * 512], ALU.max,
                                 inc=T(s_ve) if kc == nck - 1 else None)
                    prev_maxdone = s_ve.n
                    P.wait("t", s_ve, prev_maxdone)
                    for slc in range(4):
                        P.mm(bank(6, n=128, off=slc * 128),
                             maxacc[:, slc * 128:(slc + 1) * 128], identf[:, :],
                             start=(slc == 0), stop=(slc == 3),
                             inc=T(s_pe) if slc == 3 else None)
                    P.wait("v", s_pe, s_pe.n)
                    P.red(perq[:, :], bank(6), ALU.max, inc=T(s_ve))
                    P.wait("t", s_ve, s_ve.n)
                    P.mm(bank(4, n=128, p=1), perq[:, :], identf[:, :],
                         start=True, stop=True, inc=T(s_pe))
                    P.wait("v", s_pe, s_pe.n)
                    P.red(m11[0:1, 0:1], bank(4, n=128, p=1), ALU.max)
                    P.ts(m11[0:1, 1:2], m11[0:1, 0:1], -1.0, inc=T(s_ve))
                    v_m11 = s_ve.n
                    P.wait("t", s_ve, v_m11)
                    P.mm(bank(4, n=1, off=256), onesf[:, :], m11[0:1, 1:2],
                         start=True, stop=True, inc=T(s_pe))
                    P.wait("v", s_pe, s_pe.n)
                    P.cp(negm[:, :], bank(4, n=1, off=256), inc=T(s_ve))
                    v_negm = s_ve.n
                    P.wait("s", s_ve, v_negm)
                    P.wait("s", s_ve, prev_exp_reads)
                    P.wait("s", s_pe, prev_pv)
                    for gch in range(qb + 1):
                        P.act(expS[:, gch * 2048:(gch + 1) * 2048],
                              scores[:, gch * 2048:(gch + 1) * 2048], AF.Exp,
                              bias=negm[:, :],
                              inc=T(s_se) if gch == qb else None)
                    v_exp = s_se.n
                    P.wait("v", s_se, v_exp)
                    for dk in range(4):
                        kc = qb * 4 + dk
                        P.tt(expS[:, kc * 512:(kc + 1) * 512],
                             expS[:, kc * 512:(kc + 1) * 512],
                             fmask[:, dk * 512:(dk + 1) * 512], ALU.mult,
                             inc=T(s_ve) if dk == 3 else None)
                    v_mask = s_ve.n
                    P.wait("t", s_ve, v_mask)
                    P.wait("t", s_ve, v_ctx_read)
                    last_of_head = (qb == 7)
                    for kc in range(nck):
                        P.mm(bank(5), V_[:, kc * 128:(kc + 1) * 128],
                             expS[:, kc * 512:(kc + 1) * 512],
                             start=(kc == 0), stop=(kc == nck - 1))
                        P.mm(bank(7, n=512, p=1), onesb[:, 0:1],
                             expS[:, kc * 512:(kc + 1) * 512],
                             start=(kc == 0), stop=(kc == nck - 1),
                             inc=T(s_pe) if kc == nck - 1 else None)
                    prev_pv = s_pe.n
                    if last_of_head:
                        fk_ring.mark((s_pe, prev_pv))
                    prev_exp_reads = v_mask
                    P.wait("v", s_pe, prev_pv)
                    P.recip(rrow[0:1, :], bank(7, n=512, p=1), inc=T(s_ve))
                    v_rr = s_ve.n
                    P.wait("t", s_ve, v_rr)
                    P.mm(bank(4), onesf[:, :], rrow[0:1, :], start=True, stop=True,
                         inc=T(s_pe))
                    v_rb = s_pe.n
                    P.wait("s", s_pe, v_rb)
                    P.act(rb_sb[:, :], bank(4), AF.Copy, inc=T(s_se))
                    v_rbs = s_se.n
                    ssl = fc_store.guard("v")
                    cs = cstage[ssl]
                    P.wait("v", s_se, v_rbs)
                    P.tt(cs[:, :], bank(5), rb_sb[:, :], ALU.mult, inc=T(s_ve))
                    v_ctx_read = s_ve.n
                    P.wait("g", s_ve, v_ctx_read)
                    fc_store.store(ctx_d[h * 128:(h + 1) * 128,
                                         qb * 512:(qb + 1) * 512], cs)
                    it_idx += 1
                nh = h + 2
                if nh < NHEAD:
                    load_head(nh)
            for s2 in range(4):
                if fc_store.vals[s2]:
                    P.wait("y", fc_store.sems[s2], fc_store.vals[s2])
            return 0

        wo_rring = Ring(P, "wor", 2, "y")
        wo_store = StoreTrack(P, "wos", 2, "g")

        def wo_phase(src_resid, dst_resid):
            barrier_loads("y")
            ctx = [A[:, t * S:(t + 1) * S] for t in range(16)]
            for t in range(16):
                P.dma("y", ctx[t], ctx_d[t * 128:(t + 1) * 128, :], s_ld)
            v_ctx = s_ld.n
            rstage = [vB(0, 128, S, F32), vB(8192, 128, S, F32)]
            tmpf = vB(16384, 128, S, F32)
            wslot = [WT[:, 0:2048], WT[:, 2048:4096]]
            evac = {}
            for it in range(16):
                sl = wt_ring.nload % 2
                w_rdy = wt_ring.load([(wslot[sl]
                    .rearrange("p (c i) -> p c i", c=16),
                    woT.rearrange("(c p) i -> p c i", p=128)
                    [:, :, it * 128:(it + 1) * 128])])
                rsl = wo_store.guard("y")
                rs = rstage[rsl]
                if it >= 2:
                    P.wait("y", s_ve, evac[it - 2])
                r_rdy = wo_rring.load([(rs,
                    src_resid[it * 128:(it + 1) * 128, :])])
                P.wait("t", *w_rdy)
                P.wait("t", s_ld, v_ctx)
                if it >= 1:
                    P.wait("t", s_ve, evac[it - 1])
                for sh in range(2):
                    bk = sh * 4
                    for d in range(16):
                        lw = wslot[sl][:, d * 128:(d + 1) * 128]
                        for sc in range(4):
                            P.mm(bank(bk + sc), lw,
                                 ctx[d][:, sh * 2048 + sc * 512:
                                        sh * 2048 + (sc + 1) * 512],
                                 start=(d == 0), stop=(d == 15),
                                 inc=T(s_pe)
                                 if (d == 15 and sc == 3) else None)
                    v_mm = s_pe.n
                    if sh == 1:
                        wt_ring.mark((s_pe, v_mm))
                    P.wait("v", s_pe, v_mm)
                    P.wait("v", *r_rdy)
                    psg = PS[:, bk * 512:(bk + 4) * 512]
                    P.ts(tmpf[:, sh * 2048:(sh + 1) * 2048], psg,
                         biasq[:, 48 + it:48 + it + 1], None, ALU.add)
                    P.tt(rs[:, sh * 2048:(sh + 1) * 2048],
                         rs[:, sh * 2048:(sh + 1) * 2048],
                         tmpf[:, sh * 2048:(sh + 1) * 2048], ALU.add,
                         inc=T(s_ve))
                evac[it] = s_ve.n
                wo_rring.mark((s_ve, evac[it]))
                P.wait("g", s_ve, evac[it])
                wo_store.store(dst_resid[it * 128:(it + 1) * 128, :], rs)
            for s2 in range(2):
                if wo_store.vals[s2]:
                    P.wait("y", wo_store.sems[s2], wo_store.vals[s2])
            return [(wo_store.sems[s2], wo_store.vals[s2]) for s2 in range(2)
                    if wo_store.vals[s2]]

        fgw_ring = Ring(P, "fgw", 4, "g")
        fdw_ring = Ring(P, "fdw", 6, "g")
        frs_ring = Ring(P, "frs", 2, "y")
        ffn_store = StoreTrack(P, "ffs", 2, "g")

        def ffn_phase(src_resid, dst_resid, prev_stores):
            xnb = [B[:, t * 1024:(t + 1) * 1024] for t in range(16)]
            hm = [A[:, it * 1024:(it + 1) * 1024] for it in range(64)]
            mean_bf = vB(16384, 128, 1024, BF16)
            var_bf = vB(17408, 128, 1024, BF16)
            m2 = vB(18432, 128, 1024, BF16)
            sqb = vB(19456, 128, 1024, BF16)
            varf = vB(20480, 128, 1024, F32)
            tnewt = vB(22528, 128, 1024, F32)
            rs_ = [vB(20480, 128, 1024, F32), vB(22528, 128, 1024, F32)]
            t1 = vB(18432, 128, 1024, F32)
            sg = [vB(19456, 128, 1024, BF16), vB(18432, 128, 1024, BF16)]
            wslot4 = [WT[:, i * 2048:(i + 1) * 2048] for i in range(4)]
            dnslot = [WT[:, 0:4096], WT[:, 4096:8192]] + \
                     [B[:, i * 4096:(i + 1) * 4096] for i in range(4)]
            for sem, val in prev_stores:
                P.wait("g", sem, val)
                P.wait("y", sem, val)
            barrier_loads("g")
            barrier_loads("y")
            for sb in range(4):
                s0 = sb * 1024
                P.wait("g", s_pe, s_pe.n)
                for t in range(16):
                    P.dma("g", xnb[t],
                          src_resid[t * 128:(t + 1) * 128, s0:s0 + 1024],
                          s_cst)
                v_x = s_cst.n
                P.wait("t", s_cst, v_x)
                P.wait("t", s_ve, s_ve.n)
                for t in range(16):
                    for sc in range(2):
                        P.mm(bank(4 + sc), onesb[:, :],
                             xnb[t][:, sc * 512:(sc + 1) * 512],
                             start=(t == 0), stop=(t == 15),
                             inc=T(s_pe) if (t == 15 and sc == 1) else None)
                v_meanmm = s_pe.n
                P.wait("v", s_pe, v_meanmm)
                for sc in range(2):
                    P.ts(mean_bf[:, sc * 512:(sc + 1) * 512], bank(4 + sc),
                         1.0 / DM, inc=T(s_ve) if sc == 1 else None)
                v_mean = s_ve.n
                sq_mm = {}
                for t in range(16):
                    if t >= 1:
                        P.wait("v", s_pe, sq_mm[t - 1])
                    P.tt(sqb[:, :], xnb[t], xnb[t], ALU.mult, inc=T(s_ve))
                    P.wait("t", s_ve, s_ve.n)
                    if t == 0:
                        P.wait("t", s_ve, v_mean)
                    for sc in range(2):
                        P.mm(bank(4 + sc), onesb[:, :],
                             sqb[:, sc * 512:(sc + 1) * 512],
                             start=(t == 0), stop=(t == 15),
                             inc=T(s_pe) if sc == 1 else None)
                    sq_mm[t] = s_pe.n
                v_sqmm = s_pe.n
                P.wait("v", s_pe, v_sqmm)
                P.tt(m2[:, :], mean_bf[:, :], mean_bf[:, :], ALU.mult)
                for sc in range(2):
                    P.ts(var_bf[:, sc * 512:(sc + 1) * 512], bank(4 + sc),
                         1.0 / DM)
                P.tt(var_bf[:, :], var_bf[:, :], m2[:, :], ALU.subtract)
                P.ts(var_bf[:, :], var_bf[:, :], float(EPS), None, ALU.add,
                     inc=T(s_ve))
                v_var = s_ve.n
                P.wait("s", s_ve, v_var)
                P.act(var_bf[:, :], var_bf[:, :], AF.Ln)
                P.act(var_bf[:, :], var_bf[:, :], AF.Exp, scale=-0.5,
                      inc=T(s_se))
                v_r0 = s_se.n
                P.wait("v", s_se, v_r0)
                for sc in range(2):
                    P.ts(varf[:, sc * 512:(sc + 1) * 512], bank(4 + sc),
                         1.0 / DM)
                P.tt(varf[:, :], varf[:, :], m2[:, :], ALU.subtract)
                P.ts(varf[:, :], varf[:, :], float(EPS), None, ALU.add)
                P.tt(tnewt[:, :], varf[:, :], var_bf[:, :], ALU.mult)
                P.tt(tnewt[:, :], tnewt[:, :], var_bf[:, :], ALU.mult)
                P.ts(tnewt[:, :], tnewt[:, :], -0.5, 1.5, ALU.mult, ALU.add)
                P.tt(var_bf[:, :], var_bf[:, :], tnewt[:, :], ALU.mult,
                     inc=T(s_ve))
                for t in range(16):
                    P.tt(xnb[t], xnb[t], mean_bf[:, :], ALU.subtract)
                    P.tt(xnb[t], xnb[t], var_bf[:, :], ALU.mult,
                         inc=T(s_ve))
                v_xn = s_ve.n
                gu_evac = {}
                for it in range(64):
                    wd = []
                    for w8d in (g8_d, u8_d):
                        sl = fgw_ring.nload % 4
                        rdy = fgw_ring.load([(wslot4[sl], w8d[it])],
                                            fill=-128.0)
                        wd.append((sl, rdy))
                    pp = it % 2
                    bk0 = pp * 4
                    if it >= 2:
                        P.wait("t", s_ve, gu_evac[it - 2])
                    P.wait("t", s_ve, v_xn)
                    for mi in range(2):
                        sl, rdy = wd[mi]
                        P.wait("t", *rdy)
                        for d in range(16):
                            lw = wslot4[sl][:, d * 128:(d + 1) * 128]
                            for sc in range(2):
                                P.mm(bank(bk0 + 2 * mi + sc), lw,
                                     xnb[d][:, sc * 512:(sc + 1) * 512],
                                     start=(d == 0), stop=(d == 15),
                                     inc=T(s_pe)
                                     if (d == 15 and sc == 1) else None)
                        fgw_ring.mark((s_pe, s_pe.n))
                    v_mm = s_pe.n
                    P.wait("s", s_pe, v_mm)
                    P.act(sg[pp][:, :], PS[:, bk0 * 512:(bk0 + 2) * 512],
                          AF.Silu, scale=scales[:, it:it + 1], inc=T(s_se))
                    v_sg = s_se.n
                    P.wait("v", s_pe, v_mm)
                    P.ts(hm[it][:, :], PS[:, (bk0 + 2) * 512:(bk0 + 4) * 512],
                         scales[:, 64 + it:65 + it], None, ALU.mult)
                    P.wait("v", s_se, v_sg)
                    P.tt(hm[it][:, :], hm[it][:, :], sg[pp][:, :], ALU.mult,
                         inc=T(s_ve))
                    gu_evac[it] = s_ve.n
                v_gu = s_ve.n
                P.wait("t", s_ve, v_gu)
                P.wait("g", s_pe, s_pe.n)
                dn_evac = {}
                for dmt in range(16):
                    dn_rdy = []
                    for h in range(2):
                        sl = fdw_ring.nload % 6
                        rdy = fdw_ring.load(
                            [(dnslot[sl][:, q * 2048:(q + 1) * 2048],
                              dn8_d[dmt][:, h * 4096 + q * 2048:
                                         h * 4096 + (q + 1) * 2048])
                             for q in range(2)], fill=-128.0)
                        dn_rdy.append((sl, rdy))
                    bko = (dmt % 2) * 2
                    if dmt >= 2:
                        P.wait("t", s_ve, dn_evac[dmt - 2])
                    ffn_store.guard("y")
                    rsl = frs_ring.nload % 2
                    if dmt < 2:
                        P.wait("y", s_ve, v_xn)
                    r_rdy = frs_ring.load([(rs_[rsl],
                        src_resid[dmt * 128:(dmt + 1) * 128, s0:s0 + 1024])])
                    for h in range(2):
                        sl, rdy = dn_rdy[h]
                        P.wait("t", *rdy)
                        for blk in range(32):
                            lw = dnslot[sl][:, blk * 128:(blk + 1) * 128]
                            for sc in range(2):
                                P.mm(bank(bko + sc), lw,
                                     hm[h * 32 + blk][:, sc * 512:
                                                      (sc + 1) * 512],
                                     start=(h == 0 and blk == 0),
                                     stop=(h == 1 and blk == 31),
                                     inc=T(s_pe)
                                     if (blk == 31 and sc == 1) else None)
                        fdw_ring.mark((s_pe, s_pe.n))
                    v_mm = s_pe.n
                    P.wait("v", s_pe, v_mm)
                    P.ts(t1[:, :], PS[:, bko * 512:(bko + 2) * 512],
                         scales[:, 128 + dmt:129 + dmt], None, ALU.mult)
                    P.wait("v", *r_rdy)
                    P.tt(rs_[rsl], rs_[rsl], t1[:, :], ALU.add, inc=T(s_ve))
                    dn_evac[dmt] = s_ve.n
                    frs_ring.mark((s_ve, s_ve.n))
                    P.wait("g", s_ve, s_ve.n)
                    ffn_store.store(dst_resid[dmt * 128:(dmt + 1) * 128,
                                              s0:s0 + 1024], rs_[rsl])
            for s2 in range(2):
                if ffn_store.vals[s2]:
                    P.wait("y", ffn_store.sems[s2], ffn_store.vals[s2])
            return [(ffn_store.sems[s2], ffn_store.vals[s2])
                    for s2 in range(2) if ffn_store.vals[s2]]

        win_stage = StoreTrack(P, "wis", 2, "g")

        def win_phase(v_xn):
            xn = [A[:, t * S:(t + 1) * S] for t in range(16)]
            convpad = vB(0, 128, 4104, BF16)
            acca = vB(4160, 128, S, BF16)
            accb = vB(8256, 128, S, BF16)
            stage = [vB(12352, 128, S, BF16), vB(16448, 128, S, BF16)]
            P.wait("v", s_ve, v_xn)
            P.memset(convpad[:, 0:3], 0.0, inc=T(s_ve))
            wslot = [WT[:, 0:2048], WT[:, 2048:4096]]
            evac_done = {}
            g = 0
            v_lastmm = 0
            for it in range(67):
                iw = 128 if it < 66 else 64
                sl = wt_ring.nload % 2
                wsrc = winT.rearrange("(c p) i -> p c i", p=128)
                w_rdy = wt_ring.load([(wslot[sl]
                    .rearrange("p (c i) -> p c i", c=16)[:, :, 0:iw],
                    wsrc[:, :, it * 128:it * 128 + iw])])
                P.wait("t", *w_rdy)
                P.wait("t", s_ve, v_xn)
                for sh in range(2):
                    bk = (g % 2) * 4
                    if g >= 2:
                        P.wait("t", s_ve, evac_done[g - 2])
                    for d in range(16):
                        lw = wslot[sl][:, d * 128:d * 128 + iw]
                        for sc in range(4):
                            P.mm(bank(bk + sc, p=iw), lw,
                                 xn[d][:, sh * 2048 + sc * 512:
                                       sh * 2048 + (sc + 1) * 512],
                                 start=(d == 0), stop=(d == 15),
                                 inc=T(s_pe) if (d == 15 and sc == 3) else None)
                    v_mm = s_pe.n
                    if sh == 1:
                        wt_ring.mark((s_pe, v_mm))
                    v_lastmm = v_mm
                    P.wait("v", s_pe, v_mm)
                    psg = PS[0:iw, bk * 512:(bk + 4) * 512]
                    if it < 32:
                        ssl = win_stage.guard("v")
                        P.cp(stage[ssl][:, 0:2048], psg, inc=T(s_ve))
                        P.wait("g", s_ve, s_ve.n)
                        win_stage.store(zT_d[it * 128:(it + 1) * 128,
                                             sh * 2048:(sh + 1) * 2048],
                                        stage[ssl][:, 0:2048])
                        evac_done[g] = s_ve.n
                    elif it < 66:
                        P.cp(convpad[:, 3 + sh * 2048:3 + (sh + 1) * 2048], psg,
                             inc=T(s_ve))
                        evac_done[g] = s_ve.n
                        if sh == 1:
                            ct = it - 32
                            cwc = lambda w: convw[:, ct * 4 + w:ct * 4 + w + 1]
                            P.ts(acca[:, :], convpad[:, 0:S], cwc(0), None,
                                 ALU.mult)
                            for w in range(1, 4):
                                P.ts(accb[:, :], convpad[:, w:w + S], cwc(w),
                                     None, ALU.mult)
                                P.tt(acca[:, :], acca[:, :], accb[:, :],
                                     ALU.add)
                            P.ts(acca[:, :], acca[:, :], convb[:, ct:ct + 1],
                                 None, ALU.add, inc=T(s_ve))
                            v_conv = s_ve.n
                            P.wait("s", s_ve, v_conv)
                            ssl = win_stage.guard("s")
                            P.act(stage[ssl][:, 0:S], acca[:, :], AF.Silu,
                                  inc=T(s_se))
                            P.wait("g", s_se, s_se.n)
                            if ct < 32:
                                win_stage.store(
                                    xssdT_d[ct * 128:(ct + 1) * 128, :],
                                    stage[ssl][:, 0:S])
                            elif ct == 32:
                                win_stage.store(bT_d, stage[ssl][:, 0:S])
                            else:
                                win_stage.store(cT_d, stage[ssl][:, 0:S])
                            evac_done[g] = s_ve.n
                    else:
                        if sh == 1:
                            dtt = vA(8192, 64, S, F32)
                            bk0 = ((g - 1) % 2) * 4
                            psg0 = PS[0:64, bk0 * 512:(bk0 + 4) * 512]
                            P.ts(dtt[:, 0:2048], psg0,
                                 mparam[:, 0:1], None, ALU.add)
                            P.ts(dtt[:, 2048:4096], psg,
                                 mparam[:, 0:1], None, ALU.add, inc=T(s_ve))
                            evac_done[g] = s_ve.n
                            evac_done[g - 1] = s_ve.n
                    g += 1
            dtt = vA(8192, 64, S, F32)
            daa = vA(0, 64, S, F32)
            dab = vA(16384, 64, S, F32)
            scr = vA(16384, 64, S, F32)
            P.wait("s", s_pe, v_lastmm)
            P.wait("v", s_pe, v_lastmm)
            P.wait("s", s_ve, s_ve.n)
            P.act(daa[:, :], dtt[:, :], AF.Abs, inc=T(s_se))
            P.act(daa[:, :], daa[:, :], AF.Exp, scale=-1.0, inc=T(s_se))
            P.wait("v", s_se, s_se.n)
            P.ts(daa[:, :], daa[:, :], 1.0, None, ALU.add, inc=T(s_ve))
            P.wait("s", s_ve, s_ve.n)
            P.act(daa[:, :], daa[:, :], AF.Ln, inc=T(s_se))
            P.wait("v", s_se, s_se.n)
            P.ts(dab[:, :], dtt[:, :], 0.0, None, ALU.max)
            P.tt(dtt[:, :], dab[:, :], daa[:, :], ALU.add)
            P.ts(daa[:, :], dtt[:, :], mparam[:, 2:3], None, ALU.mult,
                 inc=T(s_ve))
            cur, oth = daa, dab
            for kst in (1, 2, 4, 8, 16, 32, 64, 128):
                cv = cur[:, :].rearrange("p (c s) -> p c s", c=16)
                ov = oth[:, :].rearrange("p (c s) -> p c s", c=16)
                P.tt(ov[:, :, kst:], cv[:, :, kst:], cv[:, :, 0:CH - kst],
                     ALU.add)
                P.cp(ov[:, :, 0:kst], cv[:, :, 0:kst])
                cur, oth = oth, cur
            if cur is not daa:
                P.cp(daa[:, :], cur[:, :])
            P._op("v", "tensor_copy", small[0:64, 0:1], daa[:, 0:1], inc=T(s_ve))
            v_daa = s_ve.n
            ndl = vA(57344, 64, 16, F32)
            cd = vA(57408, 64, 16, F32)
            P.ts(ndl[:, :], daa[:, :].rearrange("p (c s) -> p c s", c=16)
                 [:, :, 255:256].squeeze(2), 1.0, None, ALU.mult)
            P.wait("s", s_ve, s_ve.n)
            P.act(cd[:, :], ndl[:, :], AF.Exp, inc=T(s_se))
            decay = vA(32768, 64, S, BF16)
            for c in range(16):
                P.act(decay[:, c * CH:(c + 1) * CH],
                      daa[:, c * CH:(c + 1) * CH], AF.Exp, scale=-1.0,
                      bias=ndl[:, c:c + 1],
                      inc=T(s_se) if c == 15 else None)
            v_decay = s_se.n
            dtT = vA(40960, 128, 2048, F32)
            biasC = vA(24576, 128, 2048, F32)
            lnb = vA(57472, 128, 2048, F32)
            dtdecT = vA(36864, 128, 2048, BF16)
            decayT = vA(38912, 128, 2048, BF16)
            P.wait("t", s_ve, v_daa)
            P.wait("t", s_se, v_decay)
            tp_evac = {}
            tp2_evac = {}
            for st in range(32):
                bko = st % 2
                if st >= 2:
                    P.wait("t", s_ve, tp_evac[st - 2])
                P.mm(bank(bko, n=64, p=128), dtt[:, st * 128:(st + 1) * 128],
                     identf[0:64, 0:64], start=True, stop=False)
                P.mm(bank(bko, n=64, off=64, p=128),
                     daa[:, st * 128:(st + 1) * 128],
                     identf[0:64, 0:64], start=False, stop=True,
                     inc=T(s_pe))
                v_t1 = s_pe.n
                P.wait("v", s_pe, v_t1)
                P.cp(dtT[:, st * 64:(st + 1) * 64], bank(bko, n=64, p=128))
                P.ts(biasC[:, st * 64:(st + 1) * 64],
                     bank(bko, n=64, off=64, p=128), -1.0, None, ALU.mult,
                     inc=T(s_ve))
                tp_evac[st] = s_ve.n
            for st in range(32):
                bko = 2 + st % 2
                if st >= 2:
                    P.wait("t", s_ve, tp2_evac[st - 2])
                pbt = PS[:, bko * 512:(bko + 1) * 512].bitcast(BF16)
                P.tp(pbt[:, 0:64], decay[:, st * 128:(st + 1) * 128],
                     identb[0:64, 0:64], inc=T(s_pe))
                P.wait("v", s_pe, s_pe.n)
                P.cp(decayT[:, st * 64:(st + 1) * 64], pbt[:, 0:64],
                     inc=T(s_ve))
                tp2_evac[st] = s_ve.n
            P.wait("s", s_ve, s_ve.n)
            P.act(lnb[:, :], dtT[:, :], AF.Ln, inc=T(s_se))
            P.wait("v", s_se, s_se.n)
            P.tt(biasC[:, :], biasC[:, :], lnb[:, :], ALU.add)
            P.tt(dtdecT[:, :], dtT[:, :], decayT[:, :], ALU.mult,
                 inc=T(s_ve))
            for s2 in range(2):
                if win_stage.vals[s2]:
                    P.wait("y", win_stage.sems[s2], win_stage.vals[s2])
            return s_ve.n

        xs_ring = Ring(P, "xsr", 1, "y")
        y_store = StoreTrack(P, "yst", 4, "g")

        def ssd_phase():
            daa = vA(0, 64, S, F32)
            xssd = vA(8192, 128, 8192, BF16)
            x_tok = [vA(16384, 128, S, BF16), vA(20480, 128, S, BF16)]
            biasC = vA(24576, 128, 2048, F32)
            cdrow = vA(28672, 1, DIN, F32)
            dtdecT = vA(36864, 128, 2048, BF16)
            xdtd = [vA(40960, 128, S, BF16), vA(45056, 128, S, BF16)]
            Ssc = vA(49152, 128, DIN, F32)
            cd = vA(57408, 64, 16, F32)
            ystage = [vA(61568 + i * 256, 128, 256, BF16) for i in range(4)]
            btok = [vA(62592, 128, 128, BF16), vA(62720, 128, 128, BF16)]
            CBm = [vA(62848, 128, 256, BF16), vA(63104, 128, 256, BF16)]
            cdsmall = vA(63872, 1, 64, F32)
            bB = vB(0, 128, S, BF16)
            cB = vB(4096, 128, S, BF16)
            S_bf = vB(8192, 128, DIN, BF16)
            M0b = vB(12288, 128, 4096, BF16)
            M1b = vB(16384, 128, 4096, BF16)
            Ceb = vB(20480, 128, 4096, BF16)
            barrier_loads("y")
            P.dma("y", bB[:, :], bT_d, s_ld)
            P.dma("y", cB[:, :], cT_d, s_ld)
            v_bc = s_ld.n
            P.memset(Ssc[:, :], 0.0, inc=T(s_ve))
            v_prev_yevac = 0
            v_prev_scan = 0
            v_prev_waveB = {}
            for c in range(NCH):
                x_rdy = xs_ring.load([(xssd
                    .rearrange("p (t s) -> p t s", s=CH),
                    xssdT_d.rearrange("(t p) s -> p t s", p=128)
                    [:, :, c * CH:(c + 1) * CH])])
                P.wait("t", *x_rdy)
                if v_prev_scan:
                    P.wait("t", s_ve, v_prev_scan)
                tp_ev = {}
                for grp in range(8):
                    js, hq = grp % 2, grp // 2
                    bko = grp % 2
                    if grp >= 2:
                        P.wait("t", s_ve, tp_ev[grp - 2])
                    pbt = PS[:, bko * 512:(bko + 1) * 512].bitcast(BF16)
                    for k in range(8):
                        idx = grp * 8 + k
                        ht, js2 = idx // 2, idx % 2
                        P.tp(pbt[:, k * 128:(k + 1) * 128],
                             xssd[:, ht * 256 + js2 * 128:
                                  ht * 256 + js2 * 128 + 128],
                             identb[:, :], inc=T(s_pe) if k == 7 else None)
                    P.wait("v", s_pe, s_pe.n)
                    for k in range(8):
                        idx = grp * 8 + k
                        ht, js2 = idx // 2, idx % 2
                        P.cp(x_tok[js2][:, ht * 128:(ht + 1) * 128],
                             pbt[:, k * 128:(k + 1) * 128],
                             inc=T(s_ve) if k == 7 else None)
                    tp_ev[grp] = s_ve.n
                v_xtok = s_ve.n
                for js in range(2):
                    dslice = dtdecT[:, (c * 2 + js) * 64:(c * 2 + js + 1) * 64]
                    P.tt(xdtd[js][:, :].rearrange("p (h u) -> p h u", u=64),
                         x_tok[js][:, :].rearrange("p (h u) -> p h u", u=64),
                         dslice.unsqueeze(2).to_broadcast([128, 64, 64]),
                         ALU.mult, inc=T(s_ve) if js == 1 else None)
                v_xdtd = s_ve.n
                P.wait("t", s_ld, v_bc)
                P.wait("t", s_ve, v_xtok)
                pbt2 = PS[:, 2 * 512:3 * 512].bitcast(BF16)
                for js in range(2):
                    P.tp(pbt2[:, js * 128:(js + 1) * 128],
                         bB[:, c * CH + js * 128:c * CH + (js + 1) * 128],
                         identb[:, :], inc=T(s_pe) if js == 1 else None)
                P.wait("v", s_pe, s_pe.n)
                P.cp(btok[0][:, :], pbt2[:, 0:128])
                P.cp(btok[1][:, :], pbt2[:, 128:256], inc=T(s_ve))
                v_btok = s_ve.n
                for js in range(2):
                    P.mm(bank(6 + js, n=256), bB[:, c * CH + js * 128:
                                                 c * CH + (js + 1) * 128],
                         cB[:, c * CH:(c + 1) * CH], start=True, stop=True,
                         inc=T(s_pe))
                    P.wait("v", s_pe, s_pe.n)
                    P.tt(CBm[js][:, :], bank(6 + js, n=256),
                         cmask[:, js * 256:(js + 1) * 256], ALU.mult,
                         inc=T(s_ve))
                v_cbm = s_ve.n
                if c > 0:
                    P.cp(S_bf[:, :], Ssc[:, :], inc=T(s_ve))
                v_sbf = s_ve.n
                bc_guard = {4: (0, 0), 5: (0, 0), 6: (0, 0), 7: (0, 0)}
                v_waveB_end = 0
                for wave in range(4):
                    if v_waveB_end:
                        P.wait("v", s_pe, v_waveB_end)
                        P.wait("s", s_pe, v_waveB_end)
                    for pr2 in range(8):
                        hh0 = wave * 16 + pr2 * 2
                        bcb = 4 + (pr2 % 4)
                        g_v, g_s = bc_guard[bcb]
                        if g_v:
                            P.wait("t", s_ve, g_v)
                        if g_s:
                            P.wait("t", s_se, g_s)
                        for u in range(2):
                            sel = identf[0:64, hh0 + u:hh0 + u + 1]
                            P.mm(bank(bcb, n=256, off=u * 256),
                                 sel.to_broadcast([64, 128]),
                                 daa[:, c * CH:(c + 1) * CH], start=True,
                                 stop=True, inc=T(s_pe) if u == 1 else None)
                        v_bcst = s_pe.n
                        P.wait("v", s_pe, v_bcst)
                        P.wait("s", s_pe, v_bcst)
                        for u in range(2):
                            hh = hh0 + u
                            hl = pr2 * 2 + u
                            for js in range(2):
                                dst = M0b if js == 0 else M1b
                                P.ts(dst[:, hl * 256:(hl + 1) * 256],
                                     bank(bcb, n=256, off=u * 256),
                                     biasC[:, (2 * c + js) * 64 + hh:
                                           (2 * c + js) * 64 + hh + 1],
                                     30.0, ALU.add, ALU.min,
                                     inc=T(s_ve)
                                     if (u == 1 and js == 1) else None)
                        P.act(Ceb[:, pr2 * 512:(pr2 + 1) * 512],
                              bank(bcb, n=512), AF.Exp, inc=T(s_se))
                        bc_guard[bcb] = (s_ve.n, s_se.n)
                    v_pre = s_ve.n
                    P.wait("s", s_ve, v_pre)
                    P.act(M0b[:, :], M0b[:, :], AF.Exp)
                    P.act(M1b[:, :], M1b[:, :], AF.Exp, inc=T(s_se))
                    v_exp = s_se.n
                    P.wait("v", s_se, v_exp)
                    P.wait("v", s_ld, v_bc)
                    vM0 = M0b[:, :].rearrange("p (h i) -> p h i", h=16)
                    vM1 = M1b[:, :].rearrange("p (h i) -> p h i", h=16)
                    vCe = Ceb[:, :].rearrange("p (h i) -> p h i", h=16)
                    P.tt(vM0, vM0, CBm[0][:, :].unsqueeze(1)
                         .to_broadcast([128, 16, 256]), ALU.mult)
                    P.tt(vM1, vM1, CBm[1][:, :].unsqueeze(1)
                         .to_broadcast([128, 16, 256]), ALU.mult)
                    P.tt(vCe, vCe, cB[:, c * CH:(c + 1) * CH].unsqueeze(1)
                         .to_broadcast([128, 16, 256]), ALU.mult,
                         inc=T(s_ve))
                    v_wmult = s_ve.n
                    for bk2 in range(4):
                        guard = v_prev_waveB.get(bk2, 0)
                        if guard:
                            P.wait("t", s_ve, guard)
                        P.wait("t", s_ve, v_wmult)
                        P.wait("t", s_ve, v_xdtd)
                        for k in range(4):
                            hl = bk2 * 4 + k
                            hh = wave * 16 + hl
                            po = PS[(k % 2) * 64:(k % 2) * 64 + 64,
                                    bk2 * 512 + (k // 2) * 256:
                                    bk2 * 512 + (k // 2) * 256 + 256]
                            st0 = (k < 2)
                            sp1 = (k >= 2)
                            fin = (k == 3)
                            P.mm(po, x_tok[0][:, hh * 64:(hh + 1) * 64],
                                 M0b[:, hl * 256:(hl + 1) * 256], start=st0,
                                 stop=False)
                            P.mm(po, x_tok[1][:, hh * 64:(hh + 1) * 64],
                                 M1b[:, hl * 256:(hl + 1) * 256], start=False,
                                 stop=(c == 0 and sp1),
                                 inc=T(s_pe) if (c == 0 and fin) else None)
                            if c > 0:
                                P.wait("t", s_ve, v_sbf)
                                P.mm(po, S_bf[:, hh * 64:(hh + 1) * 64],
                                     Ceb[:, hl * 256:(hl + 1) * 256],
                                     start=False, stop=sp1,
                                     inc=T(s_pe) if fin else None)
                        v_bmm = s_pe.n
                        v_waveB_end = v_bmm
                        P.wait("v", s_pe, v_bmm)
                        for pr in range(2):
                            t_glob = (wave * 16 + bk2 * 4) // 2 + pr
                            ysl = y_store.guard("v")
                            P.ts(small[:, 0:256], xssd[:, t_glob * 256:
                                                       (t_glob + 1) * 256],
                                 dp_c[:, t_glob:t_glob + 1], None, ALU.mult)
                            P.tt(ystage[ysl][:, :],
                                 PS[:, bk2 * 512 + pr * 256:
                                    bk2 * 512 + pr * 256 + 256],
                                 small[:, 0:256], ALU.add, inc=T(s_ve))
                            v_prev_waveB[bk2] = s_ve.n
                            P.wait("g", s_ve, s_ve.n)
                            y_store.store(yT_d[t_glob * 128:(t_glob + 1) * 128,
                                               c * CH:(c + 1) * CH],
                                          ystage[ysl])
                v_yevac = s_ve.n
                v_prev_yevac = v_yevac
                if c < NCH - 1:
                    P.wait("t", s_se, s_se.n)
                    P.wait("t", s_ve, s_ve.n)
                    P.mm(bank(6, n=64, off=320, p=1), cd[:, c:c + 1],
                         identf[0:64, 0:64], start=True, stop=True,
                         inc=T(s_pe))
                    P.wait("s", s_pe, s_pe.n)
                    P.act(cdsmall[0:1, :], bank(6, n=64, off=320, p=1),
                          AF.Copy, inc=T(s_se))
                    P.wait("v", s_se, s_se.n)
                    P.ts(cdrow[0:1, :].rearrange("p (h u) -> p h u", u=64),
                         cdsmall[0:1, :].unsqueeze(2)
                         .to_broadcast([1, 64, 64]), 1.0, None, ALU.mult,
                         inc=T(s_ve))
                    v_cdrow = s_ve.n
                    P.wait("t", s_ve, v_cdrow)
                    for bq in range(8):
                        P.mm(bank(bq), onesf[:, :],
                             cdrow[0:1, bq * 512:(bq + 1) * 512], start=True,
                             stop=True, inc=T(s_pe) if bq == 7 else None)
                    P.wait("v", s_pe, s_pe.n)
                    P.tt(Ssc[:, :], Ssc[:, :], PS[:, 0:DIN], ALU.mult,
                         inc=T(s_ve))
                    v_smul = s_ve.n
                    P.wait("t", s_ve, v_smul)
                    for js in range(2):
                        P.wait("t", s_ve, v_btok)
                        for hh in range(64):
                            P.mm(PS[:, hh * 64:(hh + 1) * 64], btok[js],
                                 xdtd[js][:, hh * 64:(hh + 1) * 64],
                                 start=(js == 0 and hh % 8 == 0),
                                 stop=(js == 1 and hh % 8 == 7),
                                 inc=T(s_pe)
                                 if (js == 1 and hh == 63) else None)
                    P.wait("v", s_pe, s_pe.n)
                    P.tt(Ssc[:, :], Ssc[:, :], PS[:, 0:DIN], ALU.add,
                         inc=T(s_ve))
                    v_prev_scan = s_ve.n
                    v_prev_waveB = {}
                else:
                    v_prev_scan = v_yevac
                xs_ring.mark((s_ve, v_yevac))
            for s2 in range(4):
                if y_store.vals[s2]:
                    P.wait("y", y_store.sems[s2], y_store.vals[s2])
            return 0

        m4_yz = Ring(P, "m4yz", 1, "y")
        wo2_ring = Ring(P, "wo2", 2, "g")
        m4_rring = Ring(P, "m4r", 4, "y")
        m4_store = StoreTrack(P, "m4s", 4, "g")

        def m4_phase(src_resid, dst_resid):
            barrier_loads("y")
            barrier_loads("g")
            yblk = vA(0, 128, 16384, BF16)
            zblk = vA(16384, 128, 16384, BF16)
            yn = vA(32768, 128, 16384, BF16)
            sgz = vB(0, 128, 512, BF16)
            sq = vB(512, 128, 512, BF16)
            msq = vB(1024, 128, 512, F32)
            rsl_ = [vB(2048 + i * 1024, 128, 512, F32) for i in range(4)]
            wslot2 = [WT[:, 0:4096], WT[:, 4096:8192]]
            m4_evac = {}
            for sb in range(8):
                yz_rdy = m4_yz.load([
                    (yblk.rearrange("p (t s) -> p t s", s=512),
                     yT_d.rearrange("(t p) s -> p t s", p=128)
                     [:, :, sb * 512:(sb + 1) * 512]),
                    (zblk.rearrange("p (t s) -> p t s", s=512),
                     zT_d.rearrange("(t p) s -> p t s", p=128)
                     [:, :, sb * 512:(sb + 1) * 512])])
                P.wait("s", *yz_rdy)
                P.wait("v", *yz_rdy)
                for t in range(32):
                    P.wait("s", s_ve, s_ve.n)
                    P.act(sgz[:, :], zblk[:, t * 512:(t + 1) * 512], AF.Silu,
                          inc=T(s_se))
                    P.wait("v", s_se, s_se.n)
                    P.tt(yn[:, t * 512:(t + 1) * 512],
                         yblk[:, t * 512:(t + 1) * 512], sgz[:, :], ALU.mult)
                    if t >= 1:
                        P.wait("v", s_pe, s_pe.n)
                    P.tt(sq[:, :], yn[:, t * 512:(t + 1) * 512],
                         yn[:, t * 512:(t + 1) * 512], ALU.mult,
                         inc=T(s_ve))
                    P.wait("t", s_ve, s_ve.n)
                    P.mm(bank(7), onesb[:, :], sq[:, :], start=(t == 0),
                         stop=(t == 31), inc=T(s_pe))
                v_st = s_pe.n
                P.wait("v", s_pe, v_st)
                P.ts(msq[:, :], bank(7), 1.0 / DIN, float(EPS), ALU.mult,
                     ALU.add, inc=T(s_ve))
                P.wait("s", s_ve, s_ve.n)
                rw = vB(3072 + 4096, 128, 512, F32)
                P.act(rw[:, :], msq[:, :], AF.Ln)
                P.act(rw[:, :], rw[:, :], AF.Exp, scale=-0.5, inc=T(s_se))
                P.wait("v", s_se, s_se.n)
                rw2 = vB(3072 + 5120, 128, 512, F32)
                P.tt(rw2[:, :], msq[:, :], rw[:, :], ALU.mult)
                P.tt(rw2[:, :], rw2[:, :], rw[:, :], ALU.mult)
                P.ts(rw2[:, :], rw2[:, :], -0.5, 1.5, ALU.mult, ALU.add)
                P.tt(msq[:, :], rw[:, :], rw2[:, :], ALU.mult, inc=T(s_ve))
                P.wait("v", s_ve, s_ve.n)
                for t in range(32):
                    P.ts(yn[:, t * 512:(t + 1) * 512],
                         yn[:, t * 512:(t + 1) * 512],
                         normw_c[:, t:t + 1], None, ALU.mult)
                    P.tt(yn[:, t * 512:(t + 1) * 512],
                         yn[:, t * 512:(t + 1) * 512], msq[:, :], ALU.mult,
                         inc=T(s_ve) if t == 31 else None)
                v_yn = s_ve.n
                m4_yz.mark((s_ve, v_yn))
                for dmt in range(16):
                    sl = wo2_ring.nload % 2
                    w_rdy = wo2_ring.load([(wslot2[sl]
                        .rearrange("p (c i) -> p c i", c=32),
                        woutT.rearrange("(c p) i -> p c i", p=128)
                        [:, :, dmt * 128:(dmt + 1) * 128])])
                    bk2 = (dmt % 4) + 4 * ((dmt // 4) % 2)
                    if dmt >= 8:
                        P.wait("t", s_ve, m4_evac[dmt - 8])
                    msl = m4_store.guard("y")
                    r_rdy = m4_rring.load([(rsl_[m4_rring.nload % 4],
                        src_resid[dmt * 128:(dmt + 1) * 128,
                                  sb * 512:(sb + 1) * 512])])
                    rsl = rsl_[(m4_rring.nload - 1) % 4]
                    P.wait("t", *w_rdy)
                    P.wait("t", s_ve, v_yn)
                    for hp in range(32):
                        P.mm(bank(bk2), wslot2[sl][:, hp * 128:(hp + 1) * 128],
                             yn[:, hp * 512:(hp + 1) * 512],
                             start=(hp == 0), stop=(hp == 31),
                             inc=T(s_pe) if hp == 31 else None)
                    v_mm = s_pe.n
                    wo2_ring.mark((s_pe, v_mm))
                    P.wait("v", s_pe, v_mm)
                    P.wait("v", *r_rdy)
                    P.tt(rsl, rsl, bank(bk2), ALU.add, inc=T(s_ve))
                    m4_evac[dmt] = s_ve.n
                    m4_rring.mark((s_ve, s_ve.n))
                    P.wait("g", s_ve, s_ve.n)
                    m4_store.store(dst_resid[dmt * 128:(dmt + 1) * 128,
                                             sb * 512:(sb + 1) * 512], rsl)
            for s2 in range(4):
                if m4_store.vals[s2]:
                    P.wait("y", m4_store.sems[s2], m4_store.vals[s2])
            return [(m4_store.sems[s2], m4_store.vals[s2])
                    for s2 in range(4) if m4_store.vals[s2]]

        v_xn1 = ln_phase(x_in, 0)
        if PHASES >= 2:
            qkv_phase(v_xn1)
        if PHASES >= 3:
            flash_phase()
        if PHASES >= 4:
            wo_stores = wo_phase(x_in, xres)
        if PHASES >= 5:
            ffn_phase(xres, xres, wo_stores)
        if PHASES >= 7:
            v_xn3 = ln_phase(xres, 0)
            win_phase(v_xn3)
        if PHASES >= 8:
            ssd_phase()
        if PHASES >= 9:
            m4_stores = m4_phase(xres, xres)
        if PHASES >= 10:
            ffn_phase(xres, out_xT, m4_stores)

        if PHASES >= 10:
            tail_skip = True
        barrier_loads("y")
        if PHASES < 10:
            tmpo = vB(0, 128, S, F32)
            tail = StoreTrack(P, "tail", 1, "g")
            for t in range(16):
                P.dma("y", tmpo, xres[t * 128:(t + 1) * 128, :], s_ld)
                P.wait("g", s_ld, s_ld.n)
                tail.store(out_xT[t * 128:(t + 1) * 128, :], tmpo)
                P.wait("y", tail.sems[0], tail.vals[0])
            P.wait("y", tail.sems[0], tail.vals[0])

        blk = stack.enter_context(nc.Block())

        @blk.tensor
        def _(e):
            for f in P.q["t"]:
                f(e)

        @blk.vector
        def _(e):
            for f in P.q["v"]:
                f(e)

        @blk.scalar
        def _(e):
            for f in P.q["s"]:
                f(e)

        @blk.gpsimd
        def _(e):
            for f in P.q["g"]:
                f(e)

        @blk.sync
        def _(e):
            for f in P.q["y"]:
                f(e)

    return nc



def _prep_shared(inputs):
    f32 = np.float32
    bf16 = ml_dtypes.bfloat16
    d = {}
    d["wqT"] = np.ascontiguousarray(np.asarray(inputs["L0_attn_a_Wq"]).T).astype(f32)
    d["wkT"] = np.ascontiguousarray(np.asarray(inputs["L0_attn_a_Wk"]).T).astype(f32)
    d["wvT"] = np.ascontiguousarray(np.asarray(inputs["L0_attn_a_Wv"]).T).astype(f32)
    d["woT"] = np.ascontiguousarray(np.asarray(inputs["L0_attn_a_Wo"]).T).astype(f32)
    d["b_qkvo"] = np.ascontiguousarray(np.stack(
        [np.asarray(inputs["L0_attn_a_bq"]), np.asarray(inputs["L0_attn_a_bk"]),
         np.asarray(inputs["L0_attn_a_bv"]), np.asarray(inputs["L0_attn_a_bo"])])
        .astype(f32).reshape(4, 16, 128, 1))
    invf = 10000.0 ** (-np.arange(0, 128, 2, dtype=np.float64) / 128.0)
    t = np.arange(S, dtype=np.float64)
    fr = np.outer(t, invf)
    cos = np.cos(fr).T.astype(f32)
    sin = np.sin(fr).T.astype(f32)
    d["ropeqc"] = np.ascontiguousarray(
        np.concatenate([cos, cos], 0) * f32(SM_SCALE)).astype(bf16)
    d["ropeqs"] = np.ascontiguousarray(
        np.concatenate([-sin, sin], 0) * f32(SM_SCALE)).astype(bf16)
    d["ropekc"] = np.ascontiguousarray(np.concatenate([cos, cos], 0)).astype(bf16)
    d["ropeks"] = np.ascontiguousarray(np.concatenate([-sin, sin], 0)).astype(bf16)
    swp = np.zeros((128, 128), dtype=f32)
    for i in range(128):
        swp[i, (i + 64) % 128] = 1.0
    d["swapm"] = swp.astype(bf16)
    d["winT"] = np.ascontiguousarray(np.asarray(inputs["L0_attn_b_W_in"]).T).astype(f32)
    d["convw"] = np.ascontiguousarray(
        np.asarray(inputs["L0_attn_b_conv_w"]).astype(f32).reshape(34, 128, 4))
    d["convb"] = np.ascontiguousarray(
        np.asarray(inputs["L0_attn_b_conv_b"]).astype(f32).reshape(34, 128, 1))
    d["dtbias"] = np.asarray(inputs["L0_attn_b_dt_bias"]).astype(f32).reshape(64, 1)
    d["alog"] = np.asarray(inputs["L0_attn_b_A_log"]).astype(f32).reshape(64, 1)
    d["dp"] = np.ascontiguousarray(
        np.repeat(np.asarray(inputs["L0_attn_b_Dp"]).astype(f32), 64)
        .reshape(32, 128, 1))
    d["normw"] = np.ascontiguousarray(
        np.asarray(inputs["L0_attn_b_norm_w"]).astype(f32).reshape(32, 128, 1))
    d["woutT"] = np.ascontiguousarray(np.asarray(inputs["L0_attn_b_W_out"]).T).astype(f32)
    g8 = np.asarray(inputs["L0_ffn_gate_w"]).astype(np.uint8)
    d["g8"] = np.ascontiguousarray(
        g8.reshape(64, 128, 16, 128).transpose(0, 3, 2, 1)).reshape(64, 128, 2048)
    u8 = np.asarray(inputs["L0_ffn_up_w"]).astype(np.uint8)
    d["u8"] = np.ascontiguousarray(
        u8.reshape(64, 128, 16, 128).transpose(0, 3, 2, 1)).reshape(64, 128, 2048)
    dn = np.asarray(inputs["L0_ffn_down_w"]).astype(np.uint8)
    d["dn8"] = np.ascontiguousarray(
        dn.reshape(16, 128, 64, 128).transpose(0, 3, 2, 1)).reshape(16, 128, 8192)
    d["gsc"] = np.asarray(inputs["L0_ffn_gate_scale"]).astype(f32).reshape(64, 128, 1)
    d["usc"] = np.asarray(inputs["L0_ffn_up_scale"]).astype(f32).reshape(64, 128, 1)
    d["dsc"] = np.asarray(inputs["L0_ffn_down_scale"]).astype(f32).reshape(16, 128, 1)
    d["identb"] = np.eye(128, dtype=f32).astype(bf16)
    d["identf"] = np.eye(128, dtype=f32)
    d["onesf"] = np.ones((1, 128), dtype=f32)
    kk = np.arange(128)[:, None]
    qq = np.arange(512)[None, :]
    d["fmask"] = np.ascontiguousarray(np.stack(
        [(qq >= kk + r * 128) for r in range(4)]).astype(f32)).astype(bf16)
    jj = np.arange(128)[:, None]
    ii = np.arange(256)[None, :]
    d["cmask"] = np.ascontiguousarray(np.stack(
        [(ii >= jj + r * 128) for r in range(2)]).astype(f32)).astype(bf16)
    return d


def kernel(**inputs):
    xs = np.asarray(inputs["L0_attn_a_hidden_states"], dtype=np.float32)
    shared = _prep_shared(inputs)
    nc = build_nc()
    in_maps = []
    for b in range(8):
        m = dict(shared)
        m["x_in"] = np.ascontiguousarray(xs[b].T)
        in_maps.append(m)
    res = run_bass_kernel_spmd(nc, in_maps, core_ids=list(range(8)))
    out = np.stack([np.asarray(res.results[b]["out_xT"], dtype=np.float32).T
                    for b in range(8)])
    return out
